# version 15
# speedup vs baseline: 1.2671x; 1.0080x over previous
import math
from contextlib import ExitStack

import numpy as np
import ml_dtypes

import concourse.bass as bass
import concourse.mybir as mybir
from concourse.bass import broadcast_tensor_aps
from concourse.bass_utils import run_bass_kernel_spmd

F32 = mybir.dt.float32
F16 = mybir.dt.float16
BF16 = mybir.dt.bfloat16
AF = mybir.ActivationFunctionType
ALU = mybir.AluOpType
AX = mybir.AxisListType

NCORES = 8
D = 1024
NH = 16
HD = 64
DQ = 512
DKV = 512
FF = 4096
NE = 8
MLA_SCALE = 1.0 / math.sqrt(16 + 512 + 512)
XL_SCALE = 1.0 / 8.0
EPS = 1e-5

BF = ml_dtypes.bfloat16

_PERM = np.zeros(D, np.int64)
for _h in range(NH):
    for _j in range(32):
        _PERM[64 * _h + _j] = 64 * _h + 2 * _j
        _PERM[64 * _h + 32 + _j] = 64 * _h + 2 * _j + 1


def host_tables(S):
    t = {}
    ids = np.arange(512, dtype=np.float64)
    theta = 10000.0 ** (-2.0 * ids) / 512.0
    r = np.arange(S, dtype=np.float64)[:, None] * theta[None, :]
    t["mla_cos"] = np.cos(r).astype(BF)
    t["mla_sin"] = np.sin(r).astype(BF)
    om = 1.0 / 10000.0 ** (np.arange(0, D, 2, dtype=np.float64) / D)
    kk = np.arange(S, dtype=np.float64)
    cs = np.concatenate([np.cos(kk[None, :] * om[:, None]),
                        np.sin(kk[None, :] * om[:, None])], axis=0)
    NTT_, NQB_ = S // 128, S // 512
    t["xl_cs"] = np.ascontiguousarray(
        cs.astype(BF).reshape(8, 128, NTT_, 128).transpose(1, 2, 0, 3))
    snq = np.sin(kk[None, :] * om[:, None]).astype(BF)
    csq = np.cos(kk[None, :] * om[:, None]).astype(BF)
    t["xl_sinq"] = np.ascontiguousarray(
        snq.reshape(4, 128, NQB_, 512).transpose(1, 2, 0, 3))
    t["xl_cosq"] = np.ascontiguousarray(
        csq.reshape(4, 128, NQB_, 512).transpose(1, 2, 0, 3))
    m = np.zeros((128, 4 * 512), np.float32)
    for mm in range(4):
        p = np.arange(128)[:, None]
        j = np.arange(512)[None, :]
        m[:, mm * 512:(mm + 1) * 512] = (128 * mm + p <= j)
    t["tri"] = m.astype(BF)
    t["id_f32"] = np.eye(128, dtype=np.float32)
    t["id_bf"] = np.eye(128, dtype=np.float32).astype(BF)
    t["ones_f32"] = np.ones((1, 128), np.float32)
    t["ones_bf"] = np.ones((1, 128), np.float32).astype(BF)
    t["ones_col_bf"] = np.ones((128, 1), np.float32).astype(BF)
    return t


def prep_weights(inp):
    P = _PERM
    w = {}
    f32 = lambda a: np.ascontiguousarray(np.asarray(a, np.float32))
    w["wd_q"] = f32(inp["L0_attn_Wd_q_w"])
    w["bd_q"] = f32(inp["L0_attn_Wd_q_b"]).reshape(4, 128).T.copy()
    w["wd_kv"] = f32(inp["L0_attn_Wd_kv_w"])
    w["bd_kv"] = f32(inp["L0_attn_Wd_kv_b"]).reshape(4, 128).T.copy()
    w["wu_q"] = f32(inp["L0_attn_Wu_q_w"])[:, P]
    w["bu_q"] = f32(inp["L0_attn_Wu_q_b"])[P].reshape(1, D)
    w["wq_r"] = f32(inp["L0_attn_Wq_r_w"])[:, P]
    w["bq_r"] = f32(inp["L0_attn_Wq_r_b"])[P].reshape(1, D)
    w["wu_k"] = f32(inp["L0_attn_Wu_k_w"])[:, P]
    w["bu_k"] = f32(inp["L0_attn_Wu_k_b"])[P].reshape(1, D)
    w["wk_r"] = f32(inp["L0_attn_Wk_r_w"])[:, P]
    w["bk_r"] = f32(inp["L0_attn_Wk_r_b"])[P].reshape(1, D)
    w["wu_v"] = f32(inp["L0_attn_Wu_v_w"])
    w["bu_v"] = f32(inp["L0_attn_Wu_v_b"]).reshape(1, D)
    w["wo_a"] = f32(inp["L0_attn_Wo_w"])
    w["bo_a"] = f32(inp["L0_attn_Wo_b"]).reshape(1, D)
    w["wqkv"] = f32(inp["L0_ffn_Wqkv"])
    w["wq_t"] = np.ascontiguousarray(
        f32(inp["L0_ffn_Wqkv"])[:, 0:D].reshape(8, 128, 8, 128)
        .transpose(1, 2, 0, 3))
    w["wo_x"] = f32(inp["L0_ffn_Wo"])
    w["wrT"] = f32(inp["L0_ffn_Wr"]).T.copy()
    ubt = f32(inp["L0_ffn_u_bias"]).T
    vbt = f32(inp["L0_ffn_v_bias"]).T
    w["u_bias"] = np.vstack([ubt, ubt]).copy()
    w["v_bias"] = np.vstack([vbt, vbt]).copy()
    w["ln1_g"] = f32(inp["L0_ffn_ln1_g"]).reshape(8, 128).T.copy()
    w["ln1_b"] = f32(inp["L0_ffn_ln1_b"]).reshape(8, 128).T.copy()
    w["ln2_g"] = f32(inp["L0_ffn_ln2_g"]).reshape(8, 128).T.copy()
    w["ln2_b"] = f32(inp["L0_ffn_ln2_b"]).reshape(8, 128).T.copy()
    w["ln3_g"] = f32(inp["L0_ffn_ln3_g"]).reshape(8, 128).T.copy()
    w["ln3_b"] = f32(inp["L0_ffn_ln3_b"]).reshape(8, 128).T.copy()
    w["wff1"] = f32(inp["L0_ffn_Wff1"])
    w["bff1"] = f32(inp["L0_ffn_bff1"]).reshape(32, 128).T.copy()
    w["wff2"] = f32(inp["L0_ffn_Wff2"])
    w["bff2"] = f32(inp["L0_ffn_bff2"]).reshape(1, D)
    w["wg"] = f32(inp["L0_ffn_Wg"])
    w["we1"] = f32(inp["L0_ffn_We1"])
    w["be1"] = f32(inp["L0_ffn_be1"]).reshape(8, 32, 128).transpose(2, 0, 1).copy()
    w["we2"] = f32(inp["L0_ffn_We2"])
    w["be2"] = f32(inp["L0_ffn_be2"]).reshape(1, 8 * D)
    w["wout"] = f32(inp["L0_ffn_Wout"])
    return w


ENGS = ("tensor", "vector", "scalar", "gpsimd", "sync")


class Ev:
    __slots__ = ("sem", "val", "eng", "dma", "know")

    def __init__(self, sem, val, eng, dma, know):
        self.sem, self.val, self.eng, self.dma, self.know = sem, val, eng, dma, know


class Reg:
    __slots__ = ("name", "w", "readers", "psum")

    def __init__(self, name, psum=False):
        self.name, self.w, self.readers, self.psum = name, None, [], psum


class Sem:
    __slots__ = ("h", "n", "name")

    def __init__(self, h, name):
        self.h, self.n, self.name = h, 0, name


class KB:
    def __init__(self, nc, st):
        self.nc = nc
        self.st = st
        self.q = {e: [] for e in ENGS}
        self.know = {e: {} for e in ENGS}
        self.ct = {e: self._sem("ct_" + e) for e in ("tensor", "vector", "scalar", "gpsimd")}
        self.dqs = {}
        self.dstream_cnt = {}
        self.nops = 0

    def dma_sem(self, key):
        if key not in self.dqs:
            self.dqs[key] = self._sem("dq_" + key)
        return self.dqs[key]

    def _sem(self, name):
        return Sem(self.st.enter_context(self.nc.semaphore(name)), name)

    def _need(self, deps, ev):
        if ev is not None:
            deps.append(ev)

    def op(self, eng, build, reads=(), writes=(), queue=None, stream=None, anno=None):
        deps = []
        for r in reads:
            self._need(deps, r.w)
            if r.psum:
                for e in r.readers:
                    self._need(deps, e)
        for wr in writes:
            self._need(deps, wr.w)
            for e in wr.readers:
                self._need(deps, e)

        if queue is None:
            sem, amt, dma = self.ct[eng], 1, False
        else:
            key = queue if stream is None else queue + "_" + stream
            idx = self.dstream_cnt.get(key, 0)
            self.dstream_cnt[key] = idx + 1
            sem, amt, dma = self.dma_sem(key + "_%d" % (idx % 2)), 16, True

        know = self.know[eng]
        wmax = {}
        if dma and sem.n > 0 and know.get(sem, -1) < sem.n:
            wmax[sem] = sem.n
        for ev in deps:
            v = ev.sem.n if ev.dma else ev.val
            if not ev.dma and ev.eng == eng and eng == "tensor":
                continue
            if know.get(ev.sem, -1) >= v:
                continue
            if wmax.get(ev.sem, -1) < v:
                wmax[ev.sem] = v
        for ev in deps:
            v = ev.sem.n if ev.dma else ev.val
            if (ev.sem in wmax and wmax[ev.sem] >= v) or know.get(ev.sem, -1) >= v:
                for s2, v2 in ev.know.items():
                    if know.get(s2, -1) < v2:
                        know[s2] = v2
        for s2, v2 in wmax.items():
            if know.get(s2, -1) < v2:
                know[s2] = v2

        sem.n += amt
        myval = sem.n
        ev_out = Ev(sem, myval, eng, dma, dict(know))

        wlist = [(s.h, v) for s, v in wmax.items()]
        semh = sem.h

        def fn(e, _w=wlist, _b=build, _s=semh, _a=amt):
            for h, v in _w:
                e.wait_ge(h, v)
            ins = _b(e)
            ins.then_inc(_s, _a)

        self.q[eng].append(fn)
        self.nops += 1

        for r in reads:
            r.readers.append(ev_out)
        for wr in writes:
            wr.w = ev_out
            wr.readers = []
        return ev_out

    def raw(self, eng, build):
        self.q[eng].append(build)

    def final_waits(self):
        for name, sem in self.dqs.items():
            tot = sem.n
            if tot:
                self.raw("sync", lambda e, h=sem.h, v=tot: e.wait_ge(h, v))

    def run(self, blk):
        for e in ENGS:
            ops = self.q[e]
            getattr(blk, e)(lambda eng, _ops=ops: [f(eng) for f in _ops])




class Arena:

    def __init__(self, nc, st, total_bytes):
        self.total = total_bytes
        self.base = st.enter_context(nc.sbuf_tensor("arena", [128, total_bytes // 4], F32))
        self.free = [(0, total_bytes)]
        self.live = {}
        self.ghosts = []

    def alloc(self, name, nbytes, dtype=F32, rows=128):
        nbytes = (nbytes + 63) // 64 * 64
        small = nbytes <= 8192
        cand = reversed(list(enumerate(self.free))) if small else enumerate(self.free)
        for i, (off, sz) in cand:
            if sz >= nbytes:
                if small:
                    off = off + sz - nbytes
                    self.free[i] = (self.free[i][0], sz - nbytes)
                else:
                    self.free[i] = (off + nbytes, sz - nbytes)
                if self.free[i][1] == 0:
                    self.free.pop(i)
                ap = self.base[0:rows, off // 4:(off + nbytes) // 4]
                if dtype != F32:
                    ap = ap.bitcast(dtype)
                reg = Reg(name)
                inherited = []
                for goff, gsz, evs in self.ghosts:
                    if goff < off + nbytes and off < goff + gsz:
                        inherited.extend(evs)
                reg.readers = inherited
                buf = Buf(name, ap, reg, off, nbytes, self)
                self.live[id(buf)] = buf
                return buf
        raise MemoryError(f"arena OOM for {name} ({nbytes}B); free={self.free}")

    def release(self, buf):
        del self.live[id(buf)]
        evs = list(buf.r.readers)
        if buf.r.w is not None:
            evs.append(buf.r.w)
        self.ghosts.append((buf.off, buf.size, evs))
        ent = (buf.off, buf.size)
        self.free.append(ent)
        self.free.sort()
        merged = []
        for off, sz in self.free:
            if merged and merged[-1][0] + merged[-1][1] == off:
                merged[-1] = (merged[-1][0], merged[-1][1] + sz)
            else:
                merged.append((off, sz))
        self.free = [tuple(x) for x in merged]


class Buf:
    __slots__ = ("name", "t", "r", "off", "size", "ar")

    def __init__(self, name, t, r, off, size, ar):
        self.name, self.t, self.r, self.off, self.size, self.ar = name, t, r, off, size, ar

    def free(self):
        self.ar.release(self)


def build_program(B, S, taps=()):
    NTT = S // 128
    NQB = S // 512
    TBW = 512
    nc = bass.Bass("TRN2", target_bir_lowering=False)

    x_in = nc.dram_tensor("x", [B, S, D], F32, kind="ExternalInput").ap()
    out_d = nc.dram_tensor("out", [B, S, D], F32, kind="ExternalOutput").ap()

    WSHAPES = {
        "wd_q": (D, DQ), "bd_q": (128, 4), "wd_kv": (D, DKV), "bd_kv": (128, 4),
        "wu_q": (DQ, D), "bu_q": (1, D), "wq_r": (DQ, D), "bq_r": (1, D),
        "wu_k": (DKV, D), "bu_k": (1, D), "wk_r": (D, D), "bk_r": (1, D),
        "wu_v": (DKV, D), "bu_v": (1, D), "wo_a": (D, D), "bo_a": (1, D),
        "wqkv": (D, 3 * D), "wq_t": (128, 8, 8, 128), "wo_x": (D, D), "wrT": (D, D),
        "u_bias": (128, 16), "v_bias": (128, 16),
        "ln1_g": (128, 8), "ln1_b": (128, 8), "ln2_g": (128, 8), "ln2_b": (128, 8),
        "ln3_g": (128, 8), "ln3_b": (128, 8),
        "wff1": (D, FF), "bff1": (128, 32), "wff2": (FF, D), "bff2": (1, D),
        "wg": (D, 8), "we1": (NE, D, FF), "be1": (128, 8, 32),
        "we2": (NE, FF, D), "be2": (1, 8 * D), "wout": (D, D),
    }
    wd = {k: nc.dram_tensor("w_" + k, list(sh), F32, kind="ExternalInput").ap()
          for k, sh in WSHAPES.items()}
    TSHAPES = {
        "mla_cos": ((S, 512), BF16), "mla_sin": ((S, 512), BF16),
        "xl_cs": ((128, S // 128, 8, 128), BF16),
        "xl_sinq": ((128, S // 512, 4, 512), BF16),
        "xl_cosq": ((128, S // 512, 4, 512), BF16),
        "tri": ((128, 4 * 512), BF16), "id_f32": ((128, 128), F32),
        "id_bf": ((128, 128), BF16), "ones_f32": ((1, 128), F32),
        "ones_bf": ((1, 128), BF16), "ones_col_bf": ((128, 1), BF16),
    }
    td = {k: nc.dram_tensor("t_" + k, list(sh), dt, kind="ExternalInput").ap()
          for k, (sh, dt) in TSHAPES.items()}
    tap_d = {}
    for name, shape in taps:
        tap_d[name] = nc.dram_tensor("tap_" + name, list(shape), F32,
                                     kind="ExternalOutput").ap()

    st = ExitStack()
    with st:
        kb = KB(nc, st)
        ar = Arena(nc, st, 207 * 1024)

        idf = ar.alloc("idf", 512, F32)
        idb = ar.alloc("idb", 256, BF16)
        on1f = ar.alloc("on1f", 512, F32, rows=1)
        on1b = ar.alloc("on1b", 256, BF16, rows=1)
        bdq = ar.alloc("bdq", 16, F32)
        bdkv = ar.alloc("bdkv", 16, F32)
        ubia = ar.alloc("ubia", 64, F32)
        vbia = ar.alloc("vbia", 64, F32)
        lnp = {nm: ar.alloc(nm, 32, F32) for nm in
               ("ln1_g", "ln1_b", "ln2_g", "ln2_b", "ln3_g", "ln3_b")}
        bff1 = ar.alloc("bff1", 128, F32)
        be1 = ar.alloc("be1", 1024, F32)
        brow = {}
        wgb = ar.alloc("wgb", 2 * 8 * 8, BF16)
        sc_b = ar.alloc("sc_b", 2048, F32)
        cb1B = ar.alloc("cb1B", 64, F32)
        cbb = ar.alloc("cbb", 64, F32)
        vpB = ar.alloc("vpB", 2 * 16 * 66, BF16)
        sc_b2 = ar.alloc("sc_b2", 2048, F32)
        cb1 = ar.alloc("cb1", 64, F32)
        qtil = ar.alloc("qtil", 1024, BF16)
        vbar = ar.alloc("vbar", 1024, BF16)
        vp = ar.alloc("vp", 2 * 16 * 66, BF16)
        ctmp = ar.alloc("ctmp", 2048, F32)
        s1v = ar.alloc("s1v", 64, F32)
        s1vB = ar.alloc("s1vB", 64, F32)
        mgt = ar.alloc("mgt", 4 * NTT * 8, F32)
        cng = ar.alloc("cng", 64, F32)
        lgt = ar.alloc("lgt", 64, F32)
        gsc = ar.alloc("gsc", 64, F32)
        onc = ar.alloc("onc", 2, BF16)

        pp2, pbt, pbr = [], [], []
        for j in range(4):
            t2 = st.enter_context(nc.psum_tensor("pp%d" % j, [128, 1024], F32))
            pp2.append(t2)
            for k in range(2):
                pbt.append(t2[:, k * 512:(k + 1) * 512])
                pbr.append(Reg("pb%d" % (2 * j + k), psum=True))

        def pB(i):
            return pbt[i].bitcast(BF16)

        def dma(queue, out_ap, in_ap, reads=(), writes=(), stream=None):
            e = "sync" if queue == "sync" else "gpsimd"
            return kb.op(e, lambda g, o=out_ap, i=in_ap: g.dma_start(out=o, in_=i),
                         reads=list(reads), writes=list(writes), queue=queue,
                         stream=stream)

        def mm(bi, out_ap, lhsT, rhs, start, stop, reads=()):
            return kb.op("tensor",
                         lambda t, o=out_ap, l=lhsT, r=rhs, s=start, p=stop:
                         t.matmul(o, l, r, start=s, stop=p),
                         reads=list(reads), writes=[pbr[bi]])

        def tpose(bi, out_ap, in_ap, ident, reads=()):
            return kb.op("tensor",
                         lambda t, o=out_ap, i=in_ap, d=ident: t.transpose(o, i, d),
                         reads=list(reads), writes=[pbr[bi]])

        def act(out_ap, in_ap, func, bias=0.0, scale=1.0, reads=(), writes=()):
            return kb.op("scalar",
                         lambda s, o=out_ap, i=in_ap, f=func, b=bias, sc=scale:
                         s.activation(out=o, in_=i, func=f, bias=b, scale=sc),
                         reads=list(reads), writes=list(writes))

        def vop(meth, *aps, reads=(), writes=(), **kw):
            def fn(v, _m=meth, _a=aps, _k=kw):
                return getattr(v, _m)(*_a, **_k)
            return kb.op("vector", fn, reads=list(reads), writes=list(writes))

        def gop(meth, *aps, reads=(), writes=(), **kw):
            def fn(g, _m=meth, _a=aps, _k=kw):
                return getattr(g, _m)(*_a, **_k)
            return kb.op("gpsimd", fn, reads=list(reads), writes=list(writes))

        def wload(dst_buf, w_ap, K, M, col0=0, colw=None, queue="gpsimd"):
            C = K // 128
            colw = M if colw is None else colw
            src = w_ap.rearrange("(c p) m -> p c m", p=128)
            if colw != M or col0:
                src = src[:, :, col0:col0 + colw]
            dst = dst_buf.t[:, 0:C * colw].rearrange("p (c m) -> p c m", c=C)
            return dma(queue, dst, src, writes=[dst_buf.r])

        def tload(dst_ap, key, dst_reg, queue="sync"):
            dma(queue, dst_ap, td[key], writes=[dst_reg])

        tload(idf.t[:, :], "id_f32", idf.r)
        tload(idb.t[:, :], "id_bf", idb.r)
        tload(on1f.t[:, :], "ones_f32", on1f.r)
        tload(on1b.t[:, :], "ones_bf", on1b.r)
        tload(onc.t[:, 0:1], "ones_col_bf", onc.r)
        dma("sync", bdq.t[:, 0:4], wd["bd_q"], writes=[bdq.r])
        dma("sync", bdkv.t[:, 0:4], wd["bd_kv"], writes=[bdkv.r])
        dma("sync", ubia.t[:, 0:16], wd["u_bias"], writes=[ubia.r])
        dma("sync", vbia.t[:, 0:16], wd["v_bias"], writes=[vbia.r])
        vop("tensor_scalar_mul", ubia.t[:, 0:16], ubia.t[:, 0:16], XL_SCALE,
            reads=[ubia.r], writes=[ubia.r])
        vop("tensor_scalar_mul", vbia.t[:, 0:16], vbia.t[:, 0:16], XL_SCALE,
            reads=[vbia.r], writes=[vbia.r])
        for nm in lnp:
            dma("sync", lnp[nm].t[:, 0:8], wd[nm], writes=[lnp[nm].r])
        dma("sync", bff1.t[:, 0:32], wd["bff1"], writes=[bff1.r])
        dma("sync", be1.t[:, 0:256], wd["be1"].rearrange("p a b -> p (a b)"), writes=[be1.r])

        dma("gpsimd", wgb.t[:, :].rearrange("p (c m) -> p c m", c=8),
            wd["wg"].rearrange("(c p) m -> p c m", p=128), writes=[wgb.r])


        def ln_tile(zt_ap, zt_reg, w, dst_tm_bf, dst_tm_reg, zq=None, s1=None):
            zq = zsq if zq is None else zq
            s1 = s1v if s1 is None else s1
            inv = 1.0 / w
            vop("reduce_sum", s1.t[:, 0:1], zt_ap, axis=AX.X,
                reads=[zt_reg], writes=[s1.r])
            vop("tensor_mul", zq.t[:, 0:w], zt_ap, zt_ap,
                reads=[zt_reg], writes=[zq.r])
            vop("reduce_sum", s1.t[:, 1:2], zq.t[:, 0:w], axis=AX.X,
                reads=[zq.r], writes=[s1.r])
            vop("tensor_scalar_mul", s1.t[:, 2:3], s1.t[:, 0:1], inv,
                reads=[s1.r], writes=[s1.r])
            vop("tensor_scalar_mul", s1.t[:, 3:4], s1.t[:, 1:2], inv,
                reads=[s1.r], writes=[s1.r])
            vop("tensor_mul", s1.t[:, 4:5], s1.t[:, 2:3], s1.t[:, 2:3],
                reads=[s1.r], writes=[s1.r])
            vop("tensor_scalar", s1.t[:, 5:6], s1.t[:, 3:4], s1.t[:, 4:5],
                EPS, op0=ALU.subtract, op1=ALU.add,
                reads=[s1.r], writes=[s1.r])
            act(s1.t[:, 6:7], s1.t[:, 5:6], AF.Ln,
                reads=[s1.r], writes=[s1.r])
            act(s1.t[:, 7:8], s1.t[:, 6:7], AF.Exp, scale=-0.5,
                reads=[s1.r], writes=[s1.r])
            vop("tensor_scalar", dst_tm_bf, zt_ap, s1.t[:, 2:3], s1.t[:, 7:8],
                op0=ALU.subtract, op1=ALU.mult,
                reads=[zt_reg, s1.r], writes=[dst_tm_reg])

        pst = {"ppc": None, "rot": 0}
        NPP = 3

        def attn_head(h, nkt, emit_scores, vsrc, vsrc_reg, sidx,
                      diag_from=None, compute_c=False):
            np_ = nkt // 2
            vv = [vp, vpB][sidx]
            cb = [cb1, cb1B][sidx]
            cneg = (cng.t[0:1, 0:1], cng.r)
            cbank = 6 + sidx
            vsl = vsrc.rearrange("p (t d) -> p t d", d=D)[:, 0:nkt, h * 64:h * 64 + 64]
            vpd = vv.t[:, 0:nkt * 66].rearrange("p (t e) -> p t e", e=66)
            vop("tensor_copy", vpd[:, :, 0:64], vsl,
                reads=[vsrc_reg], writes=[vv.r])
            o3 = onc.t[:, 0:1].rearrange("p (t e) -> p t e", e=1)
            o_b, vd_b = broadcast_tensor_aps(o3, vpd[:, :, 64:65])
            vop("tensor_copy", vd_b, o_b, reads=[onc.r], writes=[vv.r])
            pend = []
            for j in range(np_):
                slot = j % 2
                kt0 = 2 * j
                emit_scores(kt0, 2 * slot)
                emit_scores(kt0 + 1, 2 * slot + 1)
                if compute_c and j == 0:
                    vop("reduce_max", cb.t[:, 0:1], pp2[slot][:, 0:1024],
                        axis=AX.X, negate=True,
                        reads=[pbr[2 * slot], pbr[2 * slot + 1]], writes=[cb.r])
                    tpose(cbank, pbt[cbank][0:1, 0:128], cb.t[:, 0:1],
                          idf.t[:, :], reads=[cb.r, idf.r])
                    vop("tensor_reduce", cneg[0], pbt[cbank][0:1, 0:128],
                        axis=AX.X, op=ALU.min, reads=[pbr[cbank]],
                        writes=[cneg[1]])
                    mm(cbank, pbt[cbank][0:128, 0:1], on1f.t[0:1, :], cneg[0],
                       True, True, reads=[on1f.r, cneg[1]])
                    vop("tensor_copy", cbb.t[:, 0:1], pbt[cbank][0:128, 0:1],
                        reads=[pbr[cbank]], writes=[cbb.r])
                pc = pst["ppc"][pst["rot"] % NPP]
                pst["rot"] += 1
                act(pc.t[:, 0:1024], pp2[slot][:, 0:1024], AF.Exp,
                    bias=cbb.t[:, 0:1],
                    reads=[pbr[2 * slot], pbr[2 * slot + 1], cbb.r],
                    writes=[pc.r])
                for q_ in range(2):
                    kt = kt0 + q_
                    if diag_from is not None and kt >= diag_from:
                        m = kt - diag_from
                        vop("tensor_mul", pc.t[:, q_ * 512:q_ * 512 + 512],
                            pc.t[:, q_ * 512:q_ * 512 + 512],
                            tri.t[:, m * 512:(m + 1) * 512],
                            reads=[pc.r, tri.r], writes=[pc.r])
                def mk_ctx(kt0=kt0, pc=pc):
                    def go():
                        for q_ in range(2):
                            kt = kt0 + q_
                            mm(4 + sidx, pbt[4 + sidx][0:65, 0:512],
                               vv.t[:, kt * 66:kt * 66 + 65],
                               pc.t[:, q_ * 512:q_ * 512 + 512],
                               kt == 0, kt == nkt - 1, reads=[vv.r, pc.r])
                    return go
                pend.append(mk_ctx())
                if len(pend) > 2:
                    pend.pop(0)()
            for f_ in pend:
                f_()

        def rope_combine(tt, dst_tm_ap, dst_tm_reg, cos_ap, sin_ap, cos_reg, sin_reg,
                         qc_ap, qc_reg, qr_ap, qr_reg, tmp_ap, tmp_reg):
            ev = lambda ap: ap.rearrange("p (h c) -> p h c", c=64)[:, :, 0:32]
            od = lambda ap: ap.rearrange("p (h c) -> p h c", c=64)[:, :, 32:64]
            c3 = cos_ap.rearrange("p (h c) -> p h c", c=32)
            s3 = sin_ap.rearrange("p (h c) -> p h c", c=32)
            vop("tensor_mul", ev(tmp_ap), ev(qr_ap), c3, reads=[qr_reg, cos_reg],
                writes=[tmp_reg])
            vop("tensor_mul", od(tmp_ap), od(qr_ap), s3, reads=[qr_reg, sin_reg],
                writes=[tmp_reg])
            vop("tensor_sub", ev(dst_tm_ap), ev(tmp_ap), od(tmp_ap),
                reads=[tmp_reg], writes=[dst_tm_reg])
            vop("tensor_add", ev(dst_tm_ap), ev(dst_tm_ap), ev(qc_ap),
                reads=[dst_tm_reg, qc_reg], writes=[dst_tm_reg])
            vop("tensor_mul", ev(tmp_ap), od(qr_ap), c3, reads=[qr_reg, cos_reg],
                writes=[tmp_reg])
            vop("tensor_mul", od(tmp_ap), ev(qr_ap), s3, reads=[qr_reg, sin_reg],
                writes=[tmp_reg])
            vop("tensor_add", od(dst_tm_ap), ev(tmp_ap), od(tmp_ap),
                reads=[tmp_reg], writes=[dst_tm_reg])
            vop("tensor_add", od(dst_tm_ap), od(dst_tm_ap), od(qc_ap),
                reads=[dst_tm_reg, qc_reg], writes=[dst_tm_reg])

        def fm_transposes(src_tm_ap, src_reg, dst_buf, tt, S_, banks=(6, 7),
                          gcol=None, bcol=None, gb=None, scale=None):
            for c in range(8):
                bi = banks[c % 2]
                tpose(bi, pB(bi)[:, 0:128], src_tm_ap[:, c * 128:(c + 1) * 128],
                      idb.t[:, :], reads=[src_reg, idb.r])
                dst_ap = dst_buf.t[:, c * S_ + tt * 128: c * S_ + tt * 128 + 128]
                if gb is None and scale is not None:
                    vop("tensor_scalar_mul", dst_ap, pB(bi)[:, 0:128], scale,
                        reads=[pbr[bi]], writes=[dst_buf.r])
                elif gb is None:
                    vop("tensor_copy", dst_ap, pB(bi)[:, 0:128],
                        reads=[pbr[bi]], writes=[dst_buf.r])
                else:
                    g_, b_ = gb
                    vop("tensor_scalar", dst_ap, pB(bi)[:, 0:128],
                        g_.t[:, c:c + 1], b_.t[:, c:c + 1],
                        op0=ALU.mult, op1=ALU.add,
                        reads=[pbr[bi], g_.r, b_.r], writes=[dst_buf.r])

        def tm_transposes(src_buf, tt, S_, dst_ap, dst_reg, banks=(6, 7)):
            for c in range(8):
                bi = banks[c % 2]
                tpose(bi, pB(bi)[:, 0:128],
                      src_buf.t[:, c * S_ + tt * 128: c * S_ + tt * 128 + 128],
                      idb.t[:, :], reads=[src_buf.r, idb.r])
                vop("tensor_copy", dst_ap[:, c * 128:(c + 1) * 128], pB(bi)[:, 0:128],
                    reads=[pbr[bi]], writes=[dst_reg])

        def ctx_finalize(h, sidx, ctx_ap, ctx_reg):
            rh = 64 * (h % 2)
            bi_ctx = 4 + sidx
            den = (sc_b.t[0:1, :], sc_b.r) if sidx == 0 else (sc_b2.t[0:1, :], sc_b2.r)
            cbank = 6 + sidx
            act(den[0][:, 0:512], pbt[bi_ctx][64:65, 0:512], AF.Ln,
                reads=[pbr[bi_ctx]], writes=[den[1]])
            act(den[0][:, 0:512], den[0][:, 0:512], AF.Exp, scale=-1.0,
                reads=[den[1]], writes=[den[1]])
            if rh == 0:
                mm(cbank, pbt[cbank][0:64, 0:512], on1f.t[0:1, 0:64],
                   den[0][:, 0:512], True, True, reads=[on1f.r, den[1]])
                act(ctmp.t[0:64, 0:512], pbt[bi_ctx][0:64, 0:512], AF.Identity,
                    reads=[pbr[bi_ctx]], writes=[ctmp.r])
                vop("tensor_mul", ctx_ap, ctmp.t[0:64, 0:512],
                    pbt[cbank][0:64, 0:512],
                    reads=[ctmp.r, pbr[cbank]], writes=[ctx_reg])
            else:
                act(ctbf.t[0:64, 0:512], pbt[bi_ctx][0:64, 0:512], AF.Identity,
                    reads=[pbr[bi_ctx]], writes=[ctbf.r])
                mm(cbank, pbt[cbank][64:128, 0:512], idb.t[0:64, 0:64],
                   ctbf.t[0:64, 0:512], True, True, reads=[idb.r, ctbf.r])
                bok = 6 + (1 - sidx)
                mm(bok, pbt[bok][64:128, 0:512], on1f.t[0:1, 0:64],
                   den[0][:, 0:512], True, True, reads=[on1f.r, den[1]])
                act(ctmp.t[64:128, 0:512], pbt[cbank][64:128, 0:512], AF.Identity,
                    reads=[pbr[cbank]], writes=[ctmp.r])
                vop("tensor_mul", ctx_ap, ctmp.t[64:128, 0:512],
                    pbt[bok][64:128, 0:512],
                    reads=[ctmp.r, pbr[bok]], writes=[ctx_reg])

        def maybe_tap(name, ap, reg):
            if name in tap_d:
                dma("gpsimd", tap_d[name], ap, reads=[reg])

        ctbf = ar.alloc("ctbf", 1024, BF16)

        def ln_alloc():
            return {"zt0": ar.alloc("zt0", 4096, F32),
                    "zt1": ar.alloc("zt1", 4096, F32),
                    "zsq": ar.alloc("zsq", 4096, F32),
                    "nrm0": ar.alloc("nrm0", 2048, BF16),
                    "nrm1": ar.alloc("nrm1", 2048, BF16)}

        for b in range(B):
            xtm = x_in[b].rearrange("(t p) d -> p t d", p=128)
            otm = out_d[b].rearrange("(t p) d -> p t d", p=128)

            for nm_ in ("bu_q", "bq_r", "bu_k", "bk_r", "bu_v", "bo_a"):
                brow[nm_] = ar.alloc("row_" + nm_, 2 * D, BF16, rows=1)
                dma("gpsimd", brow[nm_].t[0:1, :], wd[nm_], writes=[brow[nm_].r])
            xbf = ar.alloc("xbf", 2 * 8 * S, BF16)
            stgA = ar.alloc("stgA", 4 * 4 * D, F32)
            stgB = ar.alloc("stgB", 4 * 4 * D, F32)
            for g4 in range(NTT // 4):
                slot = stgA if g4 % 2 == 0 else stgB
                dma("sync", slot.t[:, :].rearrange("p (t d) -> p t d", t=4),
                    xtm[:, 4 * g4:4 * g4 + 4, :], writes=[slot.r], stream="xr")
                for tl in range(4):
                    tt = 4 * g4 + tl
                    for c in range(8):
                        bi = c % 2
                        tpose(bi, pbt[bi][:, 0:128],
                              slot.t[:, tl * D + c * 128: tl * D + c * 128 + 128],
                              idf.t[:, :], reads=[slot.r, idf.r])
                        vop("tensor_copy",
                            xbf.t[:, c * S + tt * 128: c * S + tt * 128 + 128],
                            pbt[bi][:, 0:128], reads=[pbr[bi]], writes=[xbf.r])
            stgA.free(); stgB.free()

            ctkv = ar.alloc("ctkv", 2 * 4 * S, BF16)
            wsb = ar.alloc("wsb", 2 * 8 * 512, BF16)
            wload(wsb, wd["wd_kv"], D, DKV)

            def ct_proj(dst, wsl, bias):
                for m in range(4):
                    for nb in range(S // 512):
                        bi = (m * (S // 512) + nb) % 4
                        for c in range(8):
                            mm(bi, pbt[bi][:, :],
                               wsl.t[:, c * 512 + m * 128: c * 512 + m * 128 + 128],
                               xbf.t[:, c * S + nb * 512: c * S + nb * 512 + 512],
                               c == 0, c == 7, reads=[wsl.r, xbf.r])
                        act(dst.t[:, m * S + nb * 512: m * S + nb * 512 + 512],
                            pbt[bi][:, :], AF.Identity, bias=bias.t[:, m:m + 1],
                            reads=[pbr[bi], bias.r], writes=[dst.r])

            ct_proj(ctkv, wsb, bdkv)
            wsb.free()

            key = ar.alloc("key", 2 * 8 * S, BF16)
            vtm = ar.alloc("vtm", 2 * NTT * D, BF16)
            wuk = ar.alloc("wuk", 2 * 4 * D, BF16)
            wkr = ar.alloc("wkr", 2 * 8 * D, BF16)
            wload(wuk, wd["wu_k"], DKV, D)
            wload(wkr, wd["wk_r"], D, D)
            def rope_alloc():
                d = {}
                d["mct0"] = ar.alloc("mct0", 2 * 512, BF16)
                d["mct1"] = ar.alloc("mct1", 2 * 512, BF16)
                d["mst0"] = ar.alloc("mst0", 2 * 512, BF16)
                d["mst1"] = ar.alloc("mst1", 2 * 512, BF16)
                d["qcs"] = ar.alloc("qcs", 2 * D, BF16)
                d["qrs"] = ar.alloc("qrs", 2 * D, BF16)
                d["tmps"] = ar.alloc("tmps", 2 * D, BF16)
                d["qts0"] = ar.alloc("qts0", 2 * D, BF16)
                d["qts1"] = ar.alloc("qts1", 2 * D, BF16)
                return d

            rb = rope_alloc()
            mct2, mst2 = [rb["mct0"], rb["mct1"]], [rb["mst0"], rb["mst1"]]
            qcs2 = [rb["qcs"], rb["qcs"]]
            qrs2 = [rb["qrs"], rb["qrs"]]
            qts2 = [rb["qts0"], rb["qts1"]]
            tmps2 = [rb["tmps"], rb["tmps"]]
            costm = td["mla_cos"].rearrange("(t p) f -> p t f", p=128)
            sintm = td["mla_sin"].rearrange("(t p) f -> p t f", p=128)

            def qk_tile(tt, ct_buf, w_c, brow_c, w_r, brow_r, x_lhs, dst_fm):
                sl = tt % 2
                qcs, qrs, qts, tmps = qcs2[sl], qrs2[sl], qts2[sl], tmps2[sl]
                dma("sync", mct2[sl].t[:, :], costm[:, tt, :], writes=[mct2[sl].r], stream="rt%d" % sl)
                dma("sync", mst2[sl].t[:, :], sintm[:, tt, :], writes=[mst2[sl].r], stream="rs%d" % sl)
                for half in range(2):
                    bi = half
                    for kc in range(4):
                        mm(bi, pbt[bi][:, :],
                           ct_buf.t[:, kc * S + tt * 128: kc * S + tt * 128 + 128],
                           w_c.t[:, kc * D + half * 512: kc * D + half * 512 + 512],
                           kc == 0, False, reads=[ct_buf.r, w_c.r])
                    mm(bi, pbt[bi][:, :], on1b.t[0:1, 0:128],
                       brow[brow_c].t[0:1, half * 512:half * 512 + 512],
                       False, True, reads=[on1b.r, brow[brow_c].r])
                    vop("tensor_copy", qcs.t[:, half * 512:half * 512 + 512],
                        pbt[bi][:, :], reads=[pbr[bi]], writes=[qcs.r])
                nkc = 8
                for half in range(2):
                    bi = 2 + half
                    for kc in range(nkc):
                        mm(bi, pbt[bi][:, :],
                           x_lhs.t[:, kc * S + tt * 128: kc * S + tt * 128 + 128],
                           w_r.t[:, kc * D + half * 512: kc * D + half * 512 + 512],
                           kc == 0, False, reads=[x_lhs.r, w_r.r])
                    mm(bi, pbt[bi][:, :], on1b.t[0:1, 0:128],
                       brow[brow_r].t[0:1, half * 512:half * 512 + 512],
                       False, True, reads=[on1b.r, brow[brow_r].r])
                    vop("tensor_copy", qrs.t[:, half * 512:half * 512 + 512],
                        pbt[bi][:, :], reads=[pbr[bi]], writes=[qrs.r])
                rope_combine(tt, qts.t[:, :], qts.r, mct2[sl].t[:, :],
                             mst2[sl].t[:, :],
                             mct2[sl].r, mst2[sl].r, qcs.t[:, :], qcs.r,
                             qrs.t[:, :], qrs.r, tmps.t[:, :], tmps.r)

                def ep_(qts=qts, tt=tt, dst_fm=dst_fm):
                    fm_transposes(qts.t[:, :], qts.r, dst_fm, tt, S)
                return ep_

            pend_t = None
            for tt in range(NTT):
                ep = qk_tile(tt, ctkv, wuk, "bu_k", wkr, "bk_r", xbf, key)
                if pend_t is not None:
                    pend_t()
                pend_t = ep
            pend_t()
            wuk.free(); wkr.free()
            brow["bu_k"].free(); brow["bk_r"].free()
            for v_ in rb.values():
                v_.free()
            wuv = ar.alloc("wuv", 2 * 4 * D, BF16)
            wload(wuv, wd["wu_v"], DKV, D)
            for tt in range(NTT):
                for half in range(2):
                    bi = 4 + half
                    for kc in range(4):
                        mm(bi, pbt[bi][:, :],
                           ctkv.t[:, kc * S + tt * 128: kc * S + tt * 128 + 128],
                           wuv.t[:, kc * D + half * 512: kc * D + half * 512 + 512],
                           kc == 0, False, reads=[ctkv.r, wuv.r])
                    mm(bi, pbt[bi][:, :], on1b.t[0:1, 0:128],
                       brow["bu_v"].t[0:1, half * 512:half * 512 + 512],
                       False, True, reads=[on1b.r, brow["bu_v"].r])
                    vop("tensor_copy",
                        vtm.t[:, tt * D + half * 512: tt * D + half * 512 + 512],
                        pbt[bi][:, :], reads=[pbr[bi]], writes=[vtm.r])
            wuv.free(); brow["bu_v"].free()
            ctkv.free()
            ctq = ar.alloc("ctq", 2 * 4 * S, BF16)
            wsa = ar.alloc("wsa", 2 * 8 * 512, BF16)
            wload(wsa, wd["wd_q"], D, DQ)
            ct_proj(ctq, wsa, bdq)
            wsa.free()
            xbf.free()

            x1h = ar.alloc("x1h", 2 * 8 * S, BF16)
            wuq = ar.alloc("wuq", 2 * 4 * D, BF16)
            wqr = ar.alloc("wqr", 2 * 4 * D, BF16)
            wload(wuq, wd["wu_q"], DQ, D)
            wload(wqr, wd["wq_r"], DQ, D)
            qrot = ar.alloc("qrot", 2 * 8 * 512, BF16)
            ctxq = ar.alloc("ctxq", 2 * 8 * 512, BF16)
            pst["ppc"] = [ar.alloc("ppc%d" % i, 2 * 2 * 512, BF16)
                          for i in range(NPP)]
            pst["rot"] = 0

            for qb in range(NQB):
                rb = rope_alloc()
                mct2, mst2 = [rb["mct0"], rb["mct1"]], [rb["mst0"], rb["mst1"]]
                qcs2 = [rb["qcs"], rb["qcs"]]
                qrs2 = [rb["qrs"], rb["qrs"]]
                qts2 = [rb["qts0"], rb["qts1"]]
                tmps2 = [rb["tmps"], rb["tmps"]]
                pend_t = None
                for tl in range(4):
                    tt = qb * 4 + tl
                    sl = tt % 2
                    qcs, qrs, qts, tmps = qcs2[sl], qrs2[sl], qts2[sl], tmps2[sl]
                    dma("sync", mct2[sl].t[:, :], costm[:, tt, :], writes=[mct2[sl].r], stream="rt%d" % sl)
                    dma("sync", mst2[sl].t[:, :], sintm[:, tt, :], writes=[mst2[sl].r], stream="rs%d" % sl)
                    for w_, brw, dst, bnk in ((wuq, "bu_q", qcs, 0), (wqr, "bq_r", qrs, 2)):
                        for half in range(2):
                            bi = bnk + half
                            for kc in range(4):
                                mm(bi, pbt[bi][:, :],
                                   ctq.t[:, kc * S + tt * 128: kc * S + tt * 128 + 128],
                                   w_.t[:, kc * D + half * 512: kc * D + half * 512 + 512],
                                   kc == 0, False, reads=[ctq.r, w_.r])
                            mm(bi, pbt[bi][:, :], on1b.t[0:1, 0:128],
                               brow[brw].t[0:1, half * 512:half * 512 + 512],
                               False, True, reads=[on1b.r, brow[brw].r])
                            vop("tensor_copy",
                                dst.t[:, half * 512:half * 512 + 512],
                                pbt[bi][:, :], reads=[pbr[bi]], writes=[dst.r])
                    rope_combine(tt, qts.t[:, :], qts.r, mct2[sl].t[:, :],
                                 mst2[sl].t[:, :], mct2[sl].r, mst2[sl].r,
                                 qcs.t[:, :], qcs.r,
                                 qrs.t[:, :], qrs.r, tmps.t[:, :], tmps.r)
                    if pend_t is not None:
                        pend_t()

                    def _epq(qts=qts, tl=tl):
                        fm_transposes(qts.t[:, :], qts.r, qrot, tl, 512,
                                      scale=MLA_SCALE)
                    pend_t = _epq
                pend_t()
                for v_ in rb.values():
                    v_.free()

                def mla_emit(h):
                    rh = 64 * (h % 2)
                    ch = h // 2

                    def emit_mla(kt, bi, _rh=rh, _ch=ch):
                        mm(bi, pbt[bi][:, :],
                           key.t[_rh:_rh + 64, _ch * S + kt * 128: _ch * S + kt * 128 + 128],
                           qrot.t[_rh:_rh + 64, _ch * 512:_ch * 512 + 512],
                           True, True, reads=[key.r, qrot.r])
                    return emit_mla

                for h in range(NH):
                    attn_head(h, NTT, mla_emit(h), vtm.t, vtm.r, h % 2,
                              compute_c=(h == 0))
                    if h > 0:
                        hp = h - 1
                        ctx_finalize(hp, hp % 2,
                                     ctxq.t[64 * (hp % 2):64 * (hp % 2) + 64,
                                            (hp // 2) * 512:(hp // 2) * 512 + 512],
                                     ctxq.r)
                hp = NH - 1
                ctx_finalize(hp, hp % 2,
                             ctxq.t[64 * (hp % 2):64 * (hp % 2) + 64,
                                    (hp // 2) * 512:(hp // 2) * 512 + 512],
                             ctxq.r)
                woa = ar.alloc("woa", 2 * 8 * D, BF16)
                wload(woa, wd["wo_a"], D, D)
                lnb = ln_alloc()
                zt2 = [lnb["zt0"], lnb["zt1"]]
                nrm2 = [lnb["nrm0"], lnb["nrm1"]]
                zsq = lnb["zsq"]
                s12 = [s1v, s1vB]
                vq32 = qrot.t.bitcast(F32)
                xres2 = [Buf("xres0", vq32[:, 0:1024], Reg("xres0"), 0, 0, ar),
                         Buf("xres1", vq32[:, 1024:2048], Reg("xres1"), 0, 0, ar)]
                pend_t = None
                for tl in range(4):
                    tt = qb * 4 + tl
                    sl = tl % 2
                    zt, nrm, xres = zt2[sl], nrm2[sl], xres2[sl]
                    dma("sync", xres.t[:, :], xtm[:, tt, :],
                        writes=[xres.r, qrot.r], stream="xr%d" % sl)
                    for half in range(2):
                        bi = 2 * sl + half
                        for kc in range(8):
                            mm(bi, pbt[bi][:, :],
                               ctxq.t[:, kc * 512 + tl * 128: kc * 512 + tl * 128 + 128],
                               woa.t[:, kc * D + half * 512: kc * D + half * 512 + 512],
                               kc == 0, False, reads=[ctxq.r, woa.r])
                        mm(bi, pbt[bi][:, :], on1b.t[0:1, 0:128],
                           brow["bo_a"].t[0:1, half * 512:half * 512 + 512],
                           False, True, reads=[on1b.r, brow["bo_a"].r])
                        vop("tensor_add", zt.t[:, half * 512:half * 512 + 512],
                            pbt[bi][:, :], xres.t[:, half * 512:half * 512 + 512],
                            reads=[pbr[bi], xres.r], writes=[zt.r])
                    ln_tile(zt.t[:, 0:D], zt.r, D, nrm.t[:, :], nrm.r,
                            zsq, s12[sl])
                    if pend_t is not None:
                        pend_t()

                    def _ep5(nrm=nrm, tt=tt):
                        fm_transposes(nrm.t[:, :], nrm.r, x1h, tt, S)
                    pend_t = _ep5
                pend_t()
                for v_ in lnb.values():
                    v_.free()
                woa.free()
            maybe_tap("x1h", x1h.t[:, :], x1h.r)
            ctq.free(); wuq.free(); wqr.free()
            key.free(); vtm.free()
            ctxq.free()
            for nm_ in ("bu_q", "bq_r", "bo_a"):
                brow[nm_].free()

            tri = ar.alloc("tri", 4096, BF16)
            tload(tri.t[:, :], "tri", tri.r)
            kxl = ar.alloc("kxl", 2 * 8 * S, BF16)
            vxl = ar.alloc("vxl", 2 * NTT * D, BF16)
            wkx = ar.alloc("wkx", 2 * 8 * D, BF16)
            wload(wkx, wd["wqkv"], D, 3 * D, col0=D, colw=D)
            for c in range(8):
                for nb in range(S // 512):
                    bi = (c * (S // 512) + nb) % 4
                    for kc in range(8):
                        mm(bi, pbt[bi][:, :],
                           wkx.t[:, kc * D + c * 128: kc * D + c * 128 + 128],
                           x1h.t[:, kc * S + nb * 512: kc * S + nb * 512 + 512],
                           kc == 0, kc == 7, reads=[wkx.r, x1h.r])
                    act(kxl.t[:, c * S + nb * 512: c * S + nb * 512 + 512],
                        pbt[bi][:, :], AF.Identity, reads=[pbr[bi]], writes=[kxl.r])
            wkx.free()
            wvx = ar.alloc("wvx", 2 * 8 * D, BF16)
            wload(wvx, wd["wqkv"], D, 3 * D, col0=2 * D, colw=D)
            for tt in range(NTT):
                for half in range(2):
                    bi = 2 * (tt % 2) + half
                    for kc in range(8):
                        mm(bi, pbt[bi][:, :],
                           x1h.t[:, kc * S + tt * 128: kc * S + tt * 128 + 128],
                           wvx.t[:, kc * D + half * 512: kc * D + half * 512 + 512],
                           kc == 0, kc == 7, reads=[x1h.r, wvx.r])
                    vop("tensor_copy",
                        vxl.t[:, tt * D + half * 512: tt * D + half * 512 + 512],
                        pbt[bi][:, :], reads=[pbr[bi]], writes=[vxl.r])
            wvx.free()

            ctxf = ar.alloc("ctxf", 2 * 8 * S, BF16)
            snqb = ar.alloc("snqb", 2 * 4 * 512, BF16)
            csqb = ar.alloc("csqb", 2 * 4 * 512, BF16)
            wrsl2 = [ar.alloc("wrs%d" % i, 2 * D, BF16) for i in range(2)]
            csls = [ar.alloc("csl%d" % i, 2 * 8 * 128, BF16) for i in range(2)]
            wtb = ar.alloc("wtb", 2 * 2 * 512, BF16)
            abb = ar.alloc("abb", 2 * 8 * 512, BF16)
            wtbB = ar.alloc("wtbB", 2 * 2 * 512, BF16)
            abbB = ar.alloc("abbB", 2 * 8 * 512, BF16)
            qtilB = Buf("qtilB", qtil.t, Reg("qtilB"), qtil.off, 0, ar)
            vbarB = Buf("vbarB", vbar.t, Reg("vbarB"), vbar.off, 0, ar)
            cft = ar.alloc("cft", 1024, BF16)
            cfg = ar.alloc("cfg", 1024, BF16)
            abg = Buf("abg", abb.t, Reg("abg"), abb.off, 0, ar)
            abgB = Buf("abgB", abbB.t, Reg("abgB"), abbB.off, 0, ar)
            sinq_d = td["xl_sinq"]
            cosq_d = td["xl_cosq"]
            cs_d = td["xl_cs"]
            for qb in range(NQB):
                nkt = 4 * (qb + 1)
                dma("sync", snqb.t[:, :].rearrange("p (c m) -> p c m", c=4),
                    sinq_d[:, qb, :, :], writes=[snqb.r])
                dma("sync", csqb.t[:, :].rearrange("p (c m) -> p c m", c=4),
                    cosq_d[:, qb, :, :], writes=[csqb.r])
                wqsl2 = [ar.alloc("wqs%d" % i, 2 * 8 * 128, BF16) for i in range(2)]
                for c in range(8):
                    bi = c % 2
                    wslot = c % 2
                    dma("gpsimd",
                        wqsl2[wslot].t[:, :].rearrange("p (k m) -> p k m", k=8),
                        wd["wq_t"][:, c, :, :],
                        writes=[wqsl2[wslot].r], stream="wq%d" % wslot)
                    for kc in range(8):
                        mm(bi, pbt[bi][:, :],
                           wqsl2[wslot].t[:, kc * 128: kc * 128 + 128],
                           x1h.t[:, kc * S + qb * 512: kc * S + qb * 512 + 512],
                           kc == 0, kc == 7, reads=[wqsl2[wslot].r, x1h.r])
                    act(qrot.t[:, c * 512:c * 512 + 512], pbt[bi][:, :], AF.Identity,
                        scale=XL_SCALE, reads=[pbr[bi]], writes=[qrot.r])
                wqsl2[0].free(); wqsl2[1].free()

                def xl_prep(h):
                    rh = 64 * (h % 2)
                    ch = h // 2
                    sl = h % 2
                    qt = [qtil, qtilB][sl]
                    vb = [vbar, vbarB][sl]
                    wt_ = [wtb, wtbB][sl]
                    ab_ = [abb, abbB][sl]
                    ag_ = [abg, abgB][sl]
                    cf = cft
                    vop("tensor_scalar_add", qt.t[rh:rh + 64, :],
                        qrot.t[rh:rh + 64, ch * 512:ch * 512 + 512],
                        ubia.t[rh:rh + 64, h:h + 1],
                        reads=[qrot.r, ubia.r], writes=[qt.r])
                    vop("tensor_scalar_add", vb.t[rh:rh + 64, :],
                        qrot.t[rh:rh + 64, ch * 512:ch * 512 + 512],
                        vbia.t[rh:rh + 64, h:h + 1],
                        reads=[qrot.r, vbia.r], writes=[vb.r])
                    dma("gpsimd", wrsl2[sl].t[rh:rh + 64, :],
                        wd["wrT"][64 * h:64 * h + 64, :], writes=[wrsl2[sl].r],
                        stream="wr%d" % sl)
                    for fc in range(4):
                        for j, fcx in enumerate((fc, fc + 4)):
                            bi = 2 * sl + j
                            mm(bi, pbt[bi][:, :],
                               wrsl2[sl].t[rh:rh + 64, fcx * 128:fcx * 128 + 128],
                               vb.t[rh:rh + 64, :], True, True,
                               reads=[wrsl2[sl].r, vb.r])
                        act(wt_.t[:, 0:1024], pp2[sl][:, 0:1024], AF.Identity,
                            reads=[pbr[2 * sl], pbr[2 * sl + 1]], writes=[wt_.r])
                        ws = wt_.t[:, 0:512]
                        wc = wt_.t[:, 512:1024]
                        sq_ = snqb.t[:, fc * 512:fc * 512 + 512]
                        cq_ = csqb.t[:, fc * 512:fc * 512 + 512]
                        Aap = ab_.t[:, fc * 512:fc * 512 + 512]
                        Bap = ab_.t[:, (4 + fc) * 512:(4 + fc) * 512 + 512]
                        vop("tensor_mul", Aap, ws, sq_, reads=[wt_.r, snqb.r],
                            writes=[ab_.r])
                        vop("tensor_mul", cf.t[:, :], wc, cq_,
                            reads=[wt_.r, csqb.r], writes=[cf.r])
                        vop("tensor_add", Aap, Aap, cf.t[:, :],
                            reads=[ab_.r, cf.r], writes=[ab_.r])
                        if fc < 3:
                            gop("tensor_mul", Bap, wc, sq_, reads=[wt_.r, snqb.r],
                                writes=[ag_.r])
                            gop("tensor_mul", cfg.t[:, :], ws, cq_,
                                reads=[wt_.r, csqb.r], writes=[cfg.r])
                            gop("tensor_sub", Bap, Bap, cfg.t[:, :],
                                reads=[ag_.r, cfg.r], writes=[ag_.r])
                        else:
                            vop("tensor_mul", Bap, wc, sq_, reads=[wt_.r, snqb.r],
                                writes=[ag_.r])
                            vop("tensor_mul", cf.t[:, :], ws, cq_,
                                reads=[wt_.r, csqb.r], writes=[cft.r])
                            vop("tensor_sub", Bap, Bap, cf.t[:, :],
                                reads=[ag_.r, cft.r], writes=[ag_.r])

                def xl_emit(h):
                    rh = 64 * (h % 2)
                    ch = h // 2
                    sl = h % 2
                    qt = [qtil, qtilB][sl]
                    ab_ = [abb, abbB][sl]
                    ag_ = [abg, abgB][sl]

                    def emit_xl(kt, bi, _rh=rh, _ch=ch, _qt=qt, _ab=ab_, _ag=ag_):
                        ksl = kt % 2
                        dma("sync", csls[ksl].t[:, :].rearrange("p (c m) -> p c m", c=8),
                            cs_d[:, kt, :, :], writes=[csls[ksl].r],
                            stream="cs%d" % ksl)
                        mm(bi, pbt[bi][:, :],
                           kxl.t[_rh:_rh + 64, _ch * S + kt * 128: _ch * S + kt * 128 + 128],
                           _qt.t[_rh:_rh + 64, :], True, False,
                           reads=[kxl.r, _qt.r])
                        for fc in range(8):
                            mm(bi, pbt[bi][:, :],
                               csls[ksl].t[:, fc * 128:fc * 128 + 128],
                               _ab.t[:, fc * 512:fc * 512 + 512], False, fc == 7,
                               reads=[csls[ksl].r, _ab.r if fc < 4 else _ag.r])
                    return emit_xl

                def xl_fin(hp):
                    ctx_finalize(hp, hp % 2,
                                 ctxf.t[64 * (hp % 2):64 * (hp % 2) + 64,
                                        (hp // 2) * S + qb * 512:
                                        (hp // 2) * S + qb * 512 + 512], ctxf.r)

                xl_prep(0)
                for h in range(NH):
                    if h + 1 < NH:
                        xl_prep(h + 1)
                    attn_head(h, nkt, xl_emit(h), vxl.t, vxl.r, h % 2,
                              diag_from=4 * qb, compute_c=(h == 0))
                    if h > 0:
                        xl_fin(h - 1)
                xl_fin(NH - 1)
            maybe_tap("ctxf", ctxf.t[:, :], ctxf.r)
            snqb.free(); csqb.free(); cfg.free(); cft.free()
            wrsl2[0].free(); wrsl2[1].free()
            for c_ in csls:
                c_.free()
            wtb.free(); abb.free(); wtbB.free(); abbB.free()
            for p_ in pst["ppc"]:
                p_.free()
            pst["ppc"] = None
            kxl.free(); vxl.free()
            qrot.free(); tri.free()

            x1tm2 = [ar.alloc("x1tma", 2048, BF16),
                     ar.alloc("x1tmb", 2048, BF16)]
            lnb = ln_alloc()
            zt2 = [lnb["zt0"], lnb["zt1"]]
            nrm2 = [lnb["nrm0"], lnb["nrm1"]]
            zsq = lnb["zsq"]
            s12 = [s1v, s1vB]
            h1tm4 = ar.alloc("h1tm4", 2 * 4 * D, BF16)
            h1f = ar.alloc("h1f", 2 * 8 * S, BF16)
            wox = ar.alloc("wox", 2 * 8 * D, BF16)
            wload(wox, wd["wo_x"], D, D)
            pend_t = None
            for tt in range(NTT):
                sl = tt % 2
                zt, nrm, x1tm = zt2[sl], nrm2[sl], x1tm2[sl]
                tm_transposes(x1h, tt, S, x1tm.t[:, :], x1tm.r, banks=(4, 5))
                for half in range(2):
                    bi = 2 * sl + half
                    for kc in range(8):
                        mm(bi, pbt[bi][:, :],
                           ctxf.t[:, kc * S + tt * 128: kc * S + tt * 128 + 128],
                           wox.t[:, kc * D + half * 512: kc * D + half * 512 + 512],
                           kc == 0, kc == 7, reads=[ctxf.r, wox.r])
                    vop("tensor_add", zt.t[:, half * 512:half * 512 + 512],
                        pbt[bi][:, :], x1tm.t[:, half * 512:half * 512 + 512],
                        reads=[pbr[bi], x1tm.r], writes=[zt.r])
                ln_tile(zt.t[:, 0:D], zt.r, D, nrm.t[:, :], nrm.r, zsq, s12[sl])
                if pend_t is not None:
                    pend_t()

                def _ep8(nrm=nrm, tt=tt):
                    fm_transposes(nrm.t[:, :], nrm.r, h1f, tt, S,
                                  gb=(lnp["ln1_g"], lnp["ln1_b"]))
                pend_t = _ep8
            pend_t()
            maybe_tap("h1f", h1f.t[:, :], h1f.r)
            ctxf.free(); wox.free()
            x1tm2[0].free(); x1tm2[1].free()

            h2f = ar.alloc("h2f", 2 * 8 * S, BF16)
            brow["bff2"] = ar.alloc("row_bff2", 2 * D, BF16, rows=1)
            dma("gpsimd", brow["bff2"].t[0:1, :], wd["bff2"], writes=[brow["bff2"].r])
            hid = ar.alloc("hid", 2 * 32 * TBW, BF16)
            sls = [ar.alloc("sl_%d" % i, 2 * 8 * 512, BF16) for i in range(3)]
            wff2_d = wd["wff2"].rearrange("(c p) m -> p c m", p=128)

            def ffn_like(w1_ap, w2kc_ap, bias_col0, brow2_key, brow2_off, dst_fn,
                         tb, we_bias_buf):
                pass

            for tb in range(S // TBW):
                for cb in range(8):
                    sslot = cb % 3
                    dma("gpsimd",
                        sls[sslot].t[:, :].rearrange("p (c m) -> p c m", c=8),
                        wd["wff1"].rearrange("(c p) m -> p c m", p=128)[:, :, cb * 512:cb * 512 + 512],
                        writes=[sls[sslot].r], stream="sl%d" % sslot)
                    for fc2 in range(4):
                        fcg = cb * 4 + fc2
                        for nb2 in range(TBW // 512):
                            bi = nb2
                            for kc in range(8):
                                mm(bi, pbt[bi][:, :],
                                   sls[sslot].t[:, kc * 512 + fc2 * 128: kc * 512 + fc2 * 128 + 128],
                                   h1f.t[:, kc * S + tb * TBW + nb2 * 512: kc * S + tb * TBW + nb2 * 512 + 512],
                                   kc == 0, kc == 7, reads=[sls[sslot].r, h1f.r])
                            act(hid.t[:, fcg * TBW + nb2 * 512: fcg * TBW + nb2 * 512 + 512],
                                pbt[bi][:, :], AF.Relu, bias=bff1.t[:, fcg:fcg + 1],
                                reads=[pbr[bi], bff1.r], writes=[hid.r])
                for tl2 in range(TBW // 128):
                    tt = tb * (TBW // 128) + tl2
                    tm_transposes(h1f, tt, S, h1tm4.t[:, tl2 * D:(tl2 + 1) * D],
                                  h1tm4.r, banks=(0, 1))
                for grp in range(8):
                    gslot = grp % 3
                    dma("gpsimd",
                        sls[gslot].t[:, :].rearrange("p (c m) -> p c m", c=4),
                        wff2_d[:, grp * 4:grp * 4 + 4, :], writes=[sls[gslot].r],
                        stream="sl%d" % gslot)
                    for kcl in range(4):
                        kc = grp * 4 + kcl
                        for tl2 in range(TBW // 128):
                            for half in range(2):
                                bi = tl2 * 2 + half
                                mm(bi, pbt[bi][:, :],
                                   hid.t[:, kc * TBW + tl2 * 128: kc * TBW + tl2 * 128 + 128],
                                   sls[gslot].t[:, kcl * D + half * 512: kcl * D + half * 512 + 512],
                                   kc == 0, False, reads=[hid.r, sls[gslot].r])
                pend_t = None
                for tl2 in range(TBW // 128):
                    tt = tb * (TBW // 128) + tl2
                    sl = tl2 % 2
                    zt, nrm = zt2[sl], nrm2[sl]
                    for half in range(2):
                        bi = tl2 * 2 + half
                        mm(bi, pbt[bi][:, :], on1b.t[0:1, 0:128],
                           brow["bff2"].t[0:1, half * 512:half * 512 + 512],
                           False, True, reads=[on1b.r, brow["bff2"].r])
                        vop("tensor_add", zt.t[:, half * 512:half * 512 + 512],
                            pbt[bi][:, :],
                            h1tm4.t[:, tl2 * D + half * 512: tl2 * D + half * 512 + 512],
                            reads=[pbr[bi], h1tm4.r], writes=[zt.r])
                    ln_tile(zt.t[:, 0:D], zt.r, D, nrm.t[:, :], nrm.r,
                            zsq, s12[sl])
                    if pend_t is not None:
                        pend_t()

                    def _ep9(nrm=nrm, tt=tt, tl2=tl2):
                        fm_transposes(nrm.t[:, :], nrm.r, h2f, tt, S,
                                      banks=(tl2 * 2, tl2 * 2 + 1),
                                      gb=(lnp["ln2_g"], lnp["ln2_b"]))
                    pend_t = _ep9
                pend_t()
            maybe_tap("h2f", h2f.t[:, :], h2f.r)
            h1f.free(); brow["bff2"].free(); h1tm4.free()
            for v_ in lnb.values():
                v_.free()

            for tt in range(NTT):
                for kc in range(8):
                    mm(4, pbt[4][0:128, 0:8],
                       h2f.t[:, kc * S + tt * 128: kc * S + tt * 128 + 128],
                       wgb.t[:, kc * 8:kc * 8 + 8], kc == 0, kc == 7,
                       reads=[h2f.r, wgb.r])
                vop("tensor_copy", lgt.t[:, 0:8], pbt[4][0:128, 0:8],
                    reads=[pbr[4]], writes=[lgt.r])
                vop("reduce_max", gsc.t[:, 0:1], lgt.t[:, 0:8], axis=AX.X,
                    negate=True, reads=[lgt.r], writes=[gsc.r])
                act(gsc.t[:, 4:4 + 8], lgt.t[:, 0:8], AF.Exp,
                    bias=gsc.t[:, 0:1], reads=[lgt.r, gsc.r], writes=[gsc.r])
                vop("reduce_sum", gsc.t[:, 1:2], gsc.t[:, 4:12], axis=AX.X,
                    reads=[gsc.r], writes=[gsc.r])
                vop("reciprocal", gsc.t[:, 2:3], gsc.t[:, 1:2],
                    reads=[gsc.r], writes=[gsc.r])
                vop("tensor_scalar_mul", gsc.t[:, 3:4], gsc.t[:, 0:1], -1.0,
                    reads=[gsc.r], writes=[gsc.r])
                vop("tensor_scalar", mgt.t[:, tt * 8:tt * 8 + 8], lgt.t[:, 0:8],
                    gsc.t[:, 3:4], gsc.t[:, 2:3], op0=ALU.is_equal, op1=ALU.mult,
                    reads=[lgt.r, gsc.r], writes=[mgt.r])
            maybe_tap("mgt", mgt.t[:, :], mgt.r)

            TB2 = 1024 if S >= 1024 else S
            hid.free()
            moe = ar.alloc("moe", 2 * NTT * D, BF16)
            vop("memset", moe.t[:, :], 0.0, writes=[moe.r])
            hid2p = [ar.alloc("hid2_%d" % i, 2 * 4 * TB2, BF16) for i in range(8)]

            def hid2_ap(col0, width):
                buf = hid2p[col0 // (4 * TB2)]
                off = col0 % (4 * TB2)
                return buf.t[:, off:off + width], buf
            brow["be2"] = ar.alloc("row_be2", 2 * D, BF16, rows=1)
            for e in range(NE):
                dma("gpsimd", brow["be2"].t[0:1, :],
                    wd["be2"][:, e * D:(e + 1) * D], writes=[brow["be2"].r],
                    stream="be2")
                we1_d = wd["we1"][e]
                we2_d = wd["we2"][e].rearrange("(c p) m -> p c m", p=128)
                for tb in range(S // TB2):
                    for cb in range(8):
                        sslot = cb % 3
                        dma("gpsimd",
                            sls[sslot].t[:, :].rearrange("p (c m) -> p c m", c=8),
                            we1_d.rearrange("(c p) m -> p c m", p=128)[:, :, cb * 512:cb * 512 + 512],
                            writes=[sls[sslot].r], stream="sl%d" % sslot)
                        for fc2 in range(4):
                            fcg = cb * 4 + fc2
                            for nb2 in range(TB2 // 512):
                                bi = nb2 % 2
                                for kc in range(8):
                                    mm(bi, pbt[bi][:, :],
                                       sls[sslot].t[:, kc * 512 + fc2 * 128: kc * 512 + fc2 * 128 + 128],
                                       h2f.t[:, kc * S + tb * TB2 + nb2 * 512: kc * S + tb * TB2 + nb2 * 512 + 512],
                                       kc == 0, kc == 7, reads=[sls[sslot].r, h2f.r])
                                hap, hbuf = hid2_ap(fcg * TB2 + nb2 * 512, 512)
                                act(hap, pbt[bi][:, :], AF.Relu,
                                    bias=be1.t[:, e * 32 + fcg:e * 32 + fcg + 1],
                                    reads=[pbr[bi], be1.r], writes=[hbuf.r])
                    for tg in range(TB2 // 512):
                        for grp in range(8):
                            gslot = grp % 3
                            dma("gpsimd",
                                sls[gslot].t[:, :].rearrange("p (c m) -> p c m", c=4),
                                we2_d[:, grp * 4:grp * 4 + 4, :], writes=[sls[gslot].r],
                                stream="sl%d" % gslot)
                            for kcl in range(4):
                                kc = grp * 4 + kcl
                                for tl in range(4):
                                    for half in range(2):
                                        bi = tl * 2 + half
                                        tloc = tg * 4 + tl
                                        hap, hbuf = hid2_ap(kc * TB2 + tloc * 128, 128)
                                        mm(bi, pbt[bi][:, :], hap,
                                           sls[gslot].t[:, kcl * D + half * 512: kcl * D + half * 512 + 512],
                                           kc == 0, False, reads=[hbuf.r, sls[gslot].r])
                        for tl in range(4):
                            tt = tb * (TB2 // 128) + tg * 4 + tl
                            for half in range(2):
                                bi = tl * 2 + half
                                mm(bi, pbt[bi][:, :], on1b.t[0:1, 0:128],
                                   brow["be2"].t[0:1, half * 512: half * 512 + 512],
                                   False, True, reads=[on1b.r, brow["be2"].r])
                                vop("tensor_scalar_mul", ctbf.t[:, :], pbt[bi][:, :],
                                    mgt.t[:, tt * 8 + e:tt * 8 + e + 1],
                                    reads=[pbr[bi], mgt.r], writes=[ctbf.r])
                                vop("tensor_add",
                                    moe.t[:, tt * D + half * 512: tt * D + half * 512 + 512],
                                    moe.t[:, tt * D + half * 512: tt * D + half * 512 + 512],
                                    ctbf.t[:, :], reads=[moe.r, ctbf.r], writes=[moe.r])
            for h_ in hid2p:
                h_.free()
            brow["be2"].free()
            for s_ in sls:
                s_.free()
            maybe_tap("moe", moe.t[:, :], moe.r)

            x1tm2 = [ar.alloc("x1tma", 2048, BF16),
                     ar.alloc("x1tmb", 2048, BF16)]
            h1tm2 = [ar.alloc("h1tma", 2048, BF16),
                     ar.alloc("h1tmb", 2048, BF16)]
            lnb = ln_alloc()
            zt2 = [lnb["zt0"], lnb["zt1"]]
            nrm2 = [lnb["nrm0"], lnb["nrm1"]]
            zsq = lnb["zsq"]
            s12 = [s1v, s1vB]
            outf2 = [ar.alloc("outf0", 4096, F32),
                     ar.alloc("outf1", 4096, F32)]
            wot = ar.alloc("wot", 2 * 8 * D, BF16)
            wload(wot, wd["wout"], D, D)
            h3fs2 = [ar.alloc("h3fs0", 2 * 8 * 128, BF16),
                     ar.alloc("h3fs1", 2 * 8 * 128, BF16)]
            pend_b = None
            for tt in range(NTT):
                sl = tt % 2
                zt, nrm, h1tm = zt2[sl], nrm2[sl], h1tm2[sl]
                tm_transposes(h2f, tt, S, h1tm.t[:, :], h1tm.r, banks=(0, 1))
                vop("tensor_add", zt.t[:, 0:D], h1tm.t[:, :],
                    moe.t[:, tt * D:tt * D + D], reads=[h1tm.r, moe.r],
                    writes=[zt.r])
                ln_tile(zt.t[:, 0:D], zt.r, D, nrm.t[:, :], nrm.r, zsq, s12[sl])
                if pend_b is not None:
                    pend_b()

                def _epb(tt=tt, sl=sl):
                    nrm, zt = nrm2[sl], zt2[sl]
                    h3fs, x1tm, outf = h3fs2[sl], x1tm2[sl], outf2[sl]
                    for c in range(8):
                        bi = (2, 3)[c % 2]
                        tpose(bi, pB(bi)[:, 0:128], nrm.t[:, c * 128:(c + 1) * 128],
                              idb.t[:, :], reads=[nrm.r, idb.r])
                        vop("tensor_scalar", h3fs.t[:, c * 128:(c + 1) * 128],
                            pB(bi)[:, 0:128], lnp["ln3_g"].t[:, c:c + 1],
                            lnp["ln3_b"].t[:, c:c + 1], op0=ALU.mult, op1=ALU.add,
                            reads=[pbr[bi], lnp["ln3_g"].r, lnp["ln3_b"].r],
                            writes=[h3fs.r])
                    tm_transposes(x1h, tt, S, x1tm.t[:, :], x1tm.r, banks=(6, 7))
                    for half in range(2):
                        bi = 4 + half
                        for kc in range(8):
                            mm(bi, pbt[bi][:, :],
                               h3fs.t[:, kc * 128:kc * 128 + 128],
                               wot.t[:, kc * D + half * 512: kc * D + half * 512 + 512],
                               kc == 0, kc == 7, reads=[h3fs.r, wot.r])
                        vop("tensor_add", zt.t[:, half * 512:half * 512 + 512],
                            pbt[bi][:, :], x1tm.t[:, half * 512:half * 512 + 512],
                            reads=[pbr[bi], x1tm.r], writes=[zt.r])
                    ln_tile(zt.t[:, 0:D], zt.r, D, outf.t[:, :], outf.r,
                            zsq, s12[sl])
                    dma("sync", otm[:, tt, :], outf.t[:, :], reads=[outf.r],
                        stream="out%d" % sl)
                pend_b = _epb
            pend_b()
            wot.free(); moe.free()
            for b_ in h3fs2 + x1tm2 + h1tm2 + outf2:
                b_.free()
            for v_ in lnb.values():
                v_.free()
            x1h.free(); h2f.free()

        kb.final_waits()
        with nc.Block() as blk:
            kb.run(blk)
    return nc


_PROG_CACHE = {}


def kernel(**inputs):
    x = np.asarray(inputs["L0_attn_x"], np.float32)
    BTOT, S, _ = x.shape
    BPC = BTOT // NCORES
    key = (BPC, S)
    if key not in _PROG_CACHE:
        _PROG_CACHE[key] = build_program(BPC, S)
    nc = _PROG_CACHE[key]
    w = prep_weights(inputs)
    t = host_tables(S)
    base = {}
    for k, v in w.items():
        base["w_" + k] = v
    for k, v in t.items():
        base["t_" + k] = v
    in_maps = []
    for c in range(NCORES):
        m = dict(base)
        m["x"] = np.ascontiguousarray(x[c * BPC:(c + 1) * BPC])
        in_maps.append(m)
    res = run_bass_kernel_spmd(nc, in_maps, core_ids=list(range(NCORES)))
    return np.concatenate([np.asarray(r["out"], np.float32) for r in res.results],
                          axis=0)


# version 16 (final)
# speedup vs baseline: 1.2718x; 1.0038x over previous
import math
from contextlib import ExitStack

import numpy as np
import ml_dtypes

import concourse.bass as bass
import concourse.mybir as mybir
from concourse.bass import broadcast_tensor_aps
from concourse.bass_utils import run_bass_kernel_spmd

F32 = mybir.dt.float32
F16 = mybir.dt.float16
BF16 = mybir.dt.bfloat16
AF = mybir.ActivationFunctionType
ALU = mybir.AluOpType
AX = mybir.AxisListType

NCORES = 8
D = 1024
NH = 16
HD = 64
DQ = 512
DKV = 512
FF = 4096
NE = 8
MLA_SCALE = 1.0 / math.sqrt(16 + 512 + 512)
XL_SCALE = 1.0 / 8.0
EPS = 1e-5

BF = ml_dtypes.bfloat16

_PERM = np.zeros(D, np.int64)
for _h in range(NH):
    for _j in range(32):
        _PERM[64 * _h + _j] = 64 * _h + 2 * _j
        _PERM[64 * _h + 32 + _j] = 64 * _h + 2 * _j + 1


def host_tables(S):
    t = {}
    ids = np.arange(512, dtype=np.float64)
    theta = 10000.0 ** (-2.0 * ids) / 512.0
    r = np.arange(S, dtype=np.float64)[:, None] * theta[None, :]
    t["mla_cos"] = np.cos(r).astype(BF)
    t["mla_sin"] = np.sin(r).astype(BF)
    om = 1.0 / 10000.0 ** (np.arange(0, D, 2, dtype=np.float64) / D)
    kk = np.arange(S, dtype=np.float64)
    cs = np.concatenate([np.cos(kk[None, :] * om[:, None]),
                        np.sin(kk[None, :] * om[:, None])], axis=0)
    NTT_, NQB_ = S // 128, S // 512
    t["xl_cs"] = np.ascontiguousarray(
        cs.astype(BF).reshape(8, 128, NTT_, 128).transpose(1, 2, 0, 3))
    snq = np.sin(kk[None, :] * om[:, None]).astype(BF)
    csq = np.cos(kk[None, :] * om[:, None]).astype(BF)
    t["xl_sinq"] = np.ascontiguousarray(
        snq.reshape(4, 128, NQB_, 512).transpose(1, 2, 0, 3))
    t["xl_cosq"] = np.ascontiguousarray(
        csq.reshape(4, 128, NQB_, 512).transpose(1, 2, 0, 3))
    m = np.zeros((128, 4 * 512), np.float32)
    for mm in range(4):
        p = np.arange(128)[:, None]
        j = np.arange(512)[None, :]
        m[:, mm * 512:(mm + 1) * 512] = (128 * mm + p <= j)
    t["tri"] = m.astype(BF)
    t["id_f32"] = np.eye(128, dtype=np.float32)
    t["id_bf"] = np.eye(128, dtype=np.float32).astype(BF)
    t["ones_f32"] = np.ones((1, 128), np.float32)
    t["ones_bf"] = np.ones((1, 128), np.float32).astype(BF)
    t["ones_col_bf"] = np.ones((128, 1), np.float32).astype(BF)
    return t


def prep_weights(inp):
    P = _PERM
    w = {}
    f32 = lambda a: np.ascontiguousarray(np.asarray(a, np.float32))
    w["wd_q"] = f32(inp["L0_attn_Wd_q_w"])
    w["bd_q"] = f32(inp["L0_attn_Wd_q_b"]).reshape(4, 128).T.copy()
    w["wd_kv"] = f32(inp["L0_attn_Wd_kv_w"])
    w["bd_kv"] = f32(inp["L0_attn_Wd_kv_b"]).reshape(4, 128).T.copy()
    w["wu_q"] = f32(inp["L0_attn_Wu_q_w"])[:, P]
    w["bu_q"] = f32(inp["L0_attn_Wu_q_b"])[P].reshape(1, D)
    w["wq_r"] = f32(inp["L0_attn_Wq_r_w"])[:, P]
    w["bq_r"] = f32(inp["L0_attn_Wq_r_b"])[P].reshape(1, D)
    w["wu_k"] = f32(inp["L0_attn_Wu_k_w"])[:, P]
    w["bu_k"] = f32(inp["L0_attn_Wu_k_b"])[P].reshape(1, D)
    w["wk_r"] = f32(inp["L0_attn_Wk_r_w"])[:, P]
    w["bk_r"] = f32(inp["L0_attn_Wk_r_b"])[P].reshape(1, D)
    w["wu_v"] = f32(inp["L0_attn_Wu_v_w"])
    w["bu_v"] = f32(inp["L0_attn_Wu_v_b"]).reshape(1, D)
    w["wo_a"] = f32(inp["L0_attn_Wo_w"])
    w["bo_a"] = f32(inp["L0_attn_Wo_b"]).reshape(1, D)
    w["wqkv"] = f32(inp["L0_ffn_Wqkv"])
    w["wq_t"] = np.ascontiguousarray(
        f32(inp["L0_ffn_Wqkv"])[:, 0:D].reshape(8, 128, 8, 128)
        .transpose(1, 2, 0, 3))
    w["wo_x"] = f32(inp["L0_ffn_Wo"])
    w["wrT"] = f32(inp["L0_ffn_Wr"]).T.copy()
    ubt = f32(inp["L0_ffn_u_bias"]).T
    vbt = f32(inp["L0_ffn_v_bias"]).T
    w["u_bias"] = np.vstack([ubt, ubt]).copy()
    w["v_bias"] = np.vstack([vbt, vbt]).copy()
    w["ln1_g"] = f32(inp["L0_ffn_ln1_g"]).reshape(8, 128).T.copy()
    w["ln1_b"] = f32(inp["L0_ffn_ln1_b"]).reshape(8, 128).T.copy()
    w["ln2_g"] = f32(inp["L0_ffn_ln2_g"]).reshape(8, 128).T.copy()
    w["ln2_b"] = f32(inp["L0_ffn_ln2_b"]).reshape(8, 128).T.copy()
    w["ln3_g"] = f32(inp["L0_ffn_ln3_g"]).reshape(8, 128).T.copy()
    w["ln3_b"] = f32(inp["L0_ffn_ln3_b"]).reshape(8, 128).T.copy()
    w["wff1"] = f32(inp["L0_ffn_Wff1"])
    w["bff1"] = f32(inp["L0_ffn_bff1"]).reshape(32, 128).T.copy()
    w["wff2"] = f32(inp["L0_ffn_Wff2"])
    w["bff2"] = f32(inp["L0_ffn_bff2"]).reshape(1, D)
    w["wg"] = f32(inp["L0_ffn_Wg"])
    w["we1"] = f32(inp["L0_ffn_We1"])
    w["be1"] = f32(inp["L0_ffn_be1"]).reshape(8, 32, 128).transpose(2, 0, 1).copy()
    w["we2"] = f32(inp["L0_ffn_We2"])
    w["be2"] = f32(inp["L0_ffn_be2"]).reshape(1, 8 * D)
    w["wout"] = f32(inp["L0_ffn_Wout"])
    return w


ENGS = ("tensor", "vector", "scalar", "gpsimd", "sync")


class Ev:
    __slots__ = ("sem", "val", "eng", "dma", "know")

    def __init__(self, sem, val, eng, dma, know):
        self.sem, self.val, self.eng, self.dma, self.know = sem, val, eng, dma, know


class Reg:
    __slots__ = ("name", "w", "readers", "psum")

    def __init__(self, name, psum=False):
        self.name, self.w, self.readers, self.psum = name, None, [], psum


class Sem:
    __slots__ = ("h", "n", "name")

    def __init__(self, h, name):
        self.h, self.n, self.name = h, 0, name


class KB:
    def __init__(self, nc, st):
        self.nc = nc
        self.st = st
        self.q = {e: [] for e in ENGS}
        self.know = {e: {} for e in ENGS}
        self.ct = {e: self._sem("ct_" + e) for e in ("tensor", "vector", "scalar", "gpsimd")}
        self.dqs = {}
        self.dstream_cnt = {}
        self.nops = 0

    def dma_sem(self, key):
        if key not in self.dqs:
            self.dqs[key] = self._sem("dq_" + key)
        return self.dqs[key]

    def _sem(self, name):
        return Sem(self.st.enter_context(self.nc.semaphore(name)), name)

    def _need(self, deps, ev):
        if ev is not None:
            deps.append(ev)

    def op(self, eng, build, reads=(), writes=(), queue=None, stream=None, anno=None):
        deps = []
        for r in reads:
            self._need(deps, r.w)
            if r.psum:
                for e in r.readers:
                    self._need(deps, e)
        for wr in writes:
            self._need(deps, wr.w)
            for e in wr.readers:
                self._need(deps, e)

        if queue is None:
            sem, amt, dma = self.ct[eng], 1, False
        else:
            key = queue if stream is None else queue + "_" + stream
            idx = self.dstream_cnt.get(key, 0)
            self.dstream_cnt[key] = idx + 1
            sem, amt, dma = self.dma_sem(key + "_%d" % (idx % 2)), 16, True

        know = self.know[eng]
        wmax = {}
        if dma and sem.n > 0 and know.get(sem, -1) < sem.n:
            wmax[sem] = sem.n
        for ev in deps:
            v = ev.sem.n if ev.dma else ev.val
            if not ev.dma and ev.eng == eng and eng == "tensor":
                continue
            if know.get(ev.sem, -1) >= v:
                continue
            if wmax.get(ev.sem, -1) < v:
                wmax[ev.sem] = v
        for ev in deps:
            v = ev.sem.n if ev.dma else ev.val
            if (ev.sem in wmax and wmax[ev.sem] >= v) or know.get(ev.sem, -1) >= v:
                for s2, v2 in ev.know.items():
                    if know.get(s2, -1) < v2:
                        know[s2] = v2
        for s2, v2 in wmax.items():
            if know.get(s2, -1) < v2:
                know[s2] = v2

        sem.n += amt
        myval = sem.n
        ev_out = Ev(sem, myval, eng, dma, dict(know))

        wlist = [(s.h, v) for s, v in wmax.items()]
        semh = sem.h

        def fn(e, _w=wlist, _b=build, _s=semh, _a=amt):
            for h, v in _w:
                e.wait_ge(h, v)
            ins = _b(e)
            ins.then_inc(_s, _a)

        self.q[eng].append(fn)
        self.nops += 1

        for r in reads:
            r.readers.append(ev_out)
        for wr in writes:
            wr.w = ev_out
            wr.readers = []
        return ev_out

    def raw(self, eng, build):
        self.q[eng].append(build)

    def final_waits(self):
        for name, sem in self.dqs.items():
            tot = sem.n
            if tot:
                self.raw("sync", lambda e, h=sem.h, v=tot: e.wait_ge(h, v))

    def run(self, blk):
        for e in ENGS:
            ops = self.q[e]
            getattr(blk, e)(lambda eng, _ops=ops: [f(eng) for f in _ops])




class Arena:

    def __init__(self, nc, st, total_bytes):
        self.total = total_bytes
        self.base = st.enter_context(nc.sbuf_tensor("arena", [128, total_bytes // 4], F32))
        self.free = [(0, total_bytes)]
        self.live = {}
        self.ghosts = []

    def alloc(self, name, nbytes, dtype=F32, rows=128):
        nbytes = (nbytes + 63) // 64 * 64
        small = nbytes <= 8192
        cand = reversed(list(enumerate(self.free))) if small else enumerate(self.free)
        for i, (off, sz) in cand:
            if sz >= nbytes:
                if small:
                    off = off + sz - nbytes
                    self.free[i] = (self.free[i][0], sz - nbytes)
                else:
                    self.free[i] = (off + nbytes, sz - nbytes)
                if self.free[i][1] == 0:
                    self.free.pop(i)
                ap = self.base[0:rows, off // 4:(off + nbytes) // 4]
                if dtype != F32:
                    ap = ap.bitcast(dtype)
                reg = Reg(name)
                inherited = []
                for goff, gsz, evs in self.ghosts:
                    if goff < off + nbytes and off < goff + gsz:
                        inherited.extend(evs)
                reg.readers = inherited
                buf = Buf(name, ap, reg, off, nbytes, self)
                self.live[id(buf)] = buf
                return buf
        raise MemoryError(f"arena OOM for {name} ({nbytes}B); free={self.free}")

    def release(self, buf):
        del self.live[id(buf)]
        evs = list(buf.r.readers)
        if buf.r.w is not None:
            evs.append(buf.r.w)
        self.ghosts.append((buf.off, buf.size, evs))
        ent = (buf.off, buf.size)
        self.free.append(ent)
        self.free.sort()
        merged = []
        for off, sz in self.free:
            if merged and merged[-1][0] + merged[-1][1] == off:
                merged[-1] = (merged[-1][0], merged[-1][1] + sz)
            else:
                merged.append((off, sz))
        self.free = [tuple(x) for x in merged]


class Buf:
    __slots__ = ("name", "t", "r", "off", "size", "ar")

    def __init__(self, name, t, r, off, size, ar):
        self.name, self.t, self.r, self.off, self.size, self.ar = name, t, r, off, size, ar

    def free(self):
        self.ar.release(self)


def build_program(B, S, taps=()):
    NTT = S // 128
    NQB = S // 512
    TBW = 512
    nc = bass.Bass("TRN2", target_bir_lowering=False)

    x_in = nc.dram_tensor("x", [B, S, D], F32, kind="ExternalInput").ap()
    out_d = nc.dram_tensor("out", [B, S, D], F32, kind="ExternalOutput").ap()

    WSHAPES = {
        "wd_q": (D, DQ), "bd_q": (128, 4), "wd_kv": (D, DKV), "bd_kv": (128, 4),
        "wu_q": (DQ, D), "bu_q": (1, D), "wq_r": (DQ, D), "bq_r": (1, D),
        "wu_k": (DKV, D), "bu_k": (1, D), "wk_r": (D, D), "bk_r": (1, D),
        "wu_v": (DKV, D), "bu_v": (1, D), "wo_a": (D, D), "bo_a": (1, D),
        "wqkv": (D, 3 * D), "wq_t": (128, 8, 8, 128), "wo_x": (D, D), "wrT": (D, D),
        "u_bias": (128, 16), "v_bias": (128, 16),
        "ln1_g": (128, 8), "ln1_b": (128, 8), "ln2_g": (128, 8), "ln2_b": (128, 8),
        "ln3_g": (128, 8), "ln3_b": (128, 8),
        "wff1": (D, FF), "bff1": (128, 32), "wff2": (FF, D), "bff2": (1, D),
        "wg": (D, 8), "we1": (NE, D, FF), "be1": (128, 8, 32),
        "we2": (NE, FF, D), "be2": (1, 8 * D), "wout": (D, D),
    }
    wd = {k: nc.dram_tensor("w_" + k, list(sh), F32, kind="ExternalInput").ap()
          for k, sh in WSHAPES.items()}
    TSHAPES = {
        "mla_cos": ((S, 512), BF16), "mla_sin": ((S, 512), BF16),
        "xl_cs": ((128, S // 128, 8, 128), BF16),
        "xl_sinq": ((128, S // 512, 4, 512), BF16),
        "xl_cosq": ((128, S // 512, 4, 512), BF16),
        "tri": ((128, 4 * 512), BF16), "id_f32": ((128, 128), F32),
        "id_bf": ((128, 128), BF16), "ones_f32": ((1, 128), F32),
        "ones_bf": ((1, 128), BF16), "ones_col_bf": ((128, 1), BF16),
    }
    td = {k: nc.dram_tensor("t_" + k, list(sh), dt, kind="ExternalInput").ap()
          for k, (sh, dt) in TSHAPES.items()}
    tap_d = {}
    for name, shape in taps:
        tap_d[name] = nc.dram_tensor("tap_" + name, list(shape), F32,
                                     kind="ExternalOutput").ap()

    st = ExitStack()
    with st:
        kb = KB(nc, st)
        ar = Arena(nc, st, 207 * 1024)

        idf = ar.alloc("idf", 512, F32)
        idb = ar.alloc("idb", 256, BF16)
        on1f = ar.alloc("on1f", 512, F32, rows=1)
        on1b = ar.alloc("on1b", 256, BF16, rows=1)
        bdq = ar.alloc("bdq", 16, F32)
        bdkv = ar.alloc("bdkv", 16, F32)
        ubia = ar.alloc("ubia", 64, F32)
        vbia = ar.alloc("vbia", 64, F32)
        lnp = {nm: ar.alloc(nm, 32, F32) for nm in
               ("ln1_g", "ln1_b", "ln2_g", "ln2_b", "ln3_g", "ln3_b")}
        bff1 = ar.alloc("bff1", 128, F32)
        be1 = ar.alloc("be1", 1024, F32)
        brow = {}
        wgb = ar.alloc("wgb", 2 * 8 * 8, BF16)
        sc_b = ar.alloc("sc_b", 2048, F32)
        cb1B = ar.alloc("cb1B", 64, F32)
        cbb = ar.alloc("cbb", 64, F32)
        vpB = ar.alloc("vpB", 2 * 16 * 66, BF16)
        sc_b2 = ar.alloc("sc_b2", 2048, F32)
        cb1 = ar.alloc("cb1", 64, F32)
        qtil = ar.alloc("qtil", 1024, BF16)
        vbar = ar.alloc("vbar", 1024, BF16)
        vp = ar.alloc("vp", 2 * 16 * 66, BF16)
        ctmp = ar.alloc("ctmp", 2048, F32)
        s1v = ar.alloc("s1v", 64, F32)
        s1vB = ar.alloc("s1vB", 64, F32)
        mgt = ar.alloc("mgt", 4 * NTT * 8, F32)
        cng = ar.alloc("cng", 64, F32)
        lgt = ar.alloc("lgt", 64, F32)
        gsc = ar.alloc("gsc", 64, F32)
        onc = ar.alloc("onc", 2, BF16)

        pp2, pbt, pbr = [], [], []
        for j in range(4):
            t2 = st.enter_context(nc.psum_tensor("pp%d" % j, [128, 1024], F32))
            pp2.append(t2)
            for k in range(2):
                pbt.append(t2[:, k * 512:(k + 1) * 512])
                pbr.append(Reg("pb%d" % (2 * j + k), psum=True))

        def pB(i):
            return pbt[i].bitcast(BF16)

        def dma(queue, out_ap, in_ap, reads=(), writes=(), stream=None):
            e = "sync" if queue == "sync" else "gpsimd"
            return kb.op(e, lambda g, o=out_ap, i=in_ap: g.dma_start(out=o, in_=i),
                         reads=list(reads), writes=list(writes), queue=queue,
                         stream=stream)

        def mm(bi, out_ap, lhsT, rhs, start, stop, reads=()):
            return kb.op("tensor",
                         lambda t, o=out_ap, l=lhsT, r=rhs, s=start, p=stop:
                         t.matmul(o, l, r, start=s, stop=p),
                         reads=list(reads), writes=[pbr[bi]])

        def tpose(bi, out_ap, in_ap, ident, reads=()):
            return kb.op("tensor",
                         lambda t, o=out_ap, i=in_ap, d=ident: t.transpose(o, i, d),
                         reads=list(reads), writes=[pbr[bi]])

        def act(out_ap, in_ap, func, bias=0.0, scale=1.0, reads=(), writes=()):
            return kb.op("scalar",
                         lambda s, o=out_ap, i=in_ap, f=func, b=bias, sc=scale:
                         s.activation(out=o, in_=i, func=f, bias=b, scale=sc),
                         reads=list(reads), writes=list(writes))

        def vop(meth, *aps, reads=(), writes=(), **kw):
            def fn(v, _m=meth, _a=aps, _k=kw):
                return getattr(v, _m)(*_a, **_k)
            return kb.op("vector", fn, reads=list(reads), writes=list(writes))

        def gop(meth, *aps, reads=(), writes=(), **kw):
            def fn(g, _m=meth, _a=aps, _k=kw):
                return getattr(g, _m)(*_a, **_k)
            return kb.op("gpsimd", fn, reads=list(reads), writes=list(writes))

        def wload(dst_buf, w_ap, K, M, col0=0, colw=None, queue="gpsimd"):
            C = K // 128
            colw = M if colw is None else colw
            src = w_ap.rearrange("(c p) m -> p c m", p=128)
            if colw != M or col0:
                src = src[:, :, col0:col0 + colw]
            dst = dst_buf.t[:, 0:C * colw].rearrange("p (c m) -> p c m", c=C)
            return dma(queue, dst, src, writes=[dst_buf.r])

        def tload(dst_ap, key, dst_reg, queue="sync"):
            dma(queue, dst_ap, td[key], writes=[dst_reg])

        tload(idf.t[:, :], "id_f32", idf.r)
        tload(idb.t[:, :], "id_bf", idb.r)
        tload(on1f.t[:, :], "ones_f32", on1f.r)
        tload(on1b.t[:, :], "ones_bf", on1b.r)
        tload(onc.t[:, 0:1], "ones_col_bf", onc.r)
        dma("sync", bdq.t[:, 0:4], wd["bd_q"], writes=[bdq.r])
        dma("sync", bdkv.t[:, 0:4], wd["bd_kv"], writes=[bdkv.r])
        dma("sync", ubia.t[:, 0:16], wd["u_bias"], writes=[ubia.r])
        dma("sync", vbia.t[:, 0:16], wd["v_bias"], writes=[vbia.r])
        vop("tensor_scalar_mul", ubia.t[:, 0:16], ubia.t[:, 0:16], XL_SCALE,
            reads=[ubia.r], writes=[ubia.r])
        vop("tensor_scalar_mul", vbia.t[:, 0:16], vbia.t[:, 0:16], XL_SCALE,
            reads=[vbia.r], writes=[vbia.r])
        for nm in lnp:
            dma("sync", lnp[nm].t[:, 0:8], wd[nm], writes=[lnp[nm].r])
        dma("sync", bff1.t[:, 0:32], wd["bff1"], writes=[bff1.r])
        dma("sync", be1.t[:, 0:256], wd["be1"].rearrange("p a b -> p (a b)"), writes=[be1.r])

        dma("gpsimd", wgb.t[:, :].rearrange("p (c m) -> p c m", c=8),
            wd["wg"].rearrange("(c p) m -> p c m", p=128), writes=[wgb.r])


        def ln_tile(zt_ap, zt_reg, w, dst_tm_bf, dst_tm_reg, zq=None, s1=None):
            zq = zsq if zq is None else zq
            s1 = s1v if s1 is None else s1
            inv = 1.0 / w
            vop("reduce_sum", s1.t[:, 0:1], zt_ap, axis=AX.X,
                reads=[zt_reg], writes=[s1.r])
            vop("tensor_mul", zq.t[:, 0:w], zt_ap, zt_ap,
                reads=[zt_reg], writes=[zq.r])
            vop("reduce_sum", s1.t[:, 1:2], zq.t[:, 0:w], axis=AX.X,
                reads=[zq.r], writes=[s1.r])
            vop("tensor_scalar_mul", s1.t[:, 2:3], s1.t[:, 0:1], inv,
                reads=[s1.r], writes=[s1.r])
            vop("tensor_scalar_mul", s1.t[:, 3:4], s1.t[:, 1:2], inv,
                reads=[s1.r], writes=[s1.r])
            vop("tensor_mul", s1.t[:, 4:5], s1.t[:, 2:3], s1.t[:, 2:3],
                reads=[s1.r], writes=[s1.r])
            vop("tensor_scalar", s1.t[:, 5:6], s1.t[:, 3:4], s1.t[:, 4:5],
                EPS, op0=ALU.subtract, op1=ALU.add,
                reads=[s1.r], writes=[s1.r])
            act(s1.t[:, 6:7], s1.t[:, 5:6], AF.Ln,
                reads=[s1.r], writes=[s1.r])
            act(s1.t[:, 7:8], s1.t[:, 6:7], AF.Exp, scale=-0.5,
                reads=[s1.r], writes=[s1.r])
            vop("tensor_scalar", dst_tm_bf, zt_ap, s1.t[:, 2:3], s1.t[:, 7:8],
                op0=ALU.subtract, op1=ALU.mult,
                reads=[zt_reg, s1.r], writes=[dst_tm_reg])

        pst = {"ppc": None, "rot": 0}
        NPP = 3

        def attn_head(h, nkt, emit_scores, vsrc, vsrc_reg, sidx,
                      diag_from=None, compute_c=False):
            np_ = nkt // 2
            vv = [vp, vpB][sidx]
            cb = [cb1, cb1B][sidx]
            cneg = (cng.t[0:1, 0:1], cng.r)
            cbank = 6 + sidx
            vsl = vsrc.rearrange("p (t d) -> p t d", d=D)[:, 0:nkt, h * 64:h * 64 + 64]
            vpd = vv.t[:, 0:nkt * 66].rearrange("p (t e) -> p t e", e=66)
            vop("tensor_copy", vpd[:, :, 0:64], vsl,
                reads=[vsrc_reg], writes=[vv.r])
            o3 = onc.t[:, 0:1].rearrange("p (t e) -> p t e", e=1)
            o_b, vd_b = broadcast_tensor_aps(o3, vpd[:, :, 64:65])
            vop("tensor_copy", vd_b, o_b, reads=[onc.r], writes=[vv.r])
            pend = []
            for j in range(np_):
                slot = j % 2
                kt0 = 2 * j
                emit_scores(kt0, 2 * slot)
                emit_scores(kt0 + 1, 2 * slot + 1)
                if compute_c and j == 0:
                    vop("reduce_max", cb.t[:, 0:1], pp2[slot][:, 0:1024],
                        axis=AX.X, negate=True,
                        reads=[pbr[2 * slot], pbr[2 * slot + 1]], writes=[cb.r])
                    tpose(cbank, pbt[cbank][0:1, 0:128], cb.t[:, 0:1],
                          idf.t[:, :], reads=[cb.r, idf.r])
                    vop("tensor_reduce", cneg[0], pbt[cbank][0:1, 0:128],
                        axis=AX.X, op=ALU.min, reads=[pbr[cbank]],
                        writes=[cneg[1]])
                    mm(cbank, pbt[cbank][0:128, 0:1], on1f.t[0:1, :], cneg[0],
                       True, True, reads=[on1f.r, cneg[1]])
                    vop("tensor_copy", cbb.t[:, 0:1], pbt[cbank][0:128, 0:1],
                        reads=[pbr[cbank]], writes=[cbb.r])
                pc = pst["ppc"][pst["rot"] % NPP]
                pst["rot"] += 1
                act(pc.t[:, 0:1024], pp2[slot][:, 0:1024], AF.Exp,
                    bias=cbb.t[:, 0:1],
                    reads=[pbr[2 * slot], pbr[2 * slot + 1], cbb.r],
                    writes=[pc.r])
                for q_ in range(2):
                    kt = kt0 + q_
                    if diag_from is not None and kt >= diag_from:
                        m = kt - diag_from
                        vop("tensor_mul", pc.t[:, q_ * 512:q_ * 512 + 512],
                            pc.t[:, q_ * 512:q_ * 512 + 512],
                            tri.t[:, m * 512:(m + 1) * 512],
                            reads=[pc.r, tri.r], writes=[pc.r])
                def mk_ctx(kt0=kt0, pc=pc):
                    def go():
                        for q_ in range(2):
                            kt = kt0 + q_
                            mm(4 + sidx, pbt[4 + sidx][0:65, 0:512],
                               vv.t[:, kt * 66:kt * 66 + 65],
                               pc.t[:, q_ * 512:q_ * 512 + 512],
                               kt == 0, kt == nkt - 1, reads=[vv.r, pc.r])
                    return go
                pend.append(mk_ctx())
                if len(pend) > 2:
                    pend.pop(0)()
            for f_ in pend:
                f_()

        def rope_combine(tt, dst_tm_ap, dst_tm_reg, cos_ap, sin_ap, cos_reg, sin_reg,
                         qc_ap, qc_reg, qr_ap, qr_reg, tmp_ap, tmp_reg):
            ev = lambda ap: ap.rearrange("p (h c) -> p h c", c=64)[:, :, 0:32]
            od = lambda ap: ap.rearrange("p (h c) -> p h c", c=64)[:, :, 32:64]
            c3 = cos_ap.rearrange("p (h c) -> p h c", c=32)
            s3 = sin_ap.rearrange("p (h c) -> p h c", c=32)
            vop("tensor_mul", ev(tmp_ap), ev(qr_ap), c3, reads=[qr_reg, cos_reg],
                writes=[tmp_reg])
            vop("tensor_mul", od(tmp_ap), od(qr_ap), s3, reads=[qr_reg, sin_reg],
                writes=[tmp_reg])
            vop("tensor_sub", ev(dst_tm_ap), ev(tmp_ap), od(tmp_ap),
                reads=[tmp_reg], writes=[dst_tm_reg])
            vop("tensor_add", ev(dst_tm_ap), ev(dst_tm_ap), ev(qc_ap),
                reads=[dst_tm_reg, qc_reg], writes=[dst_tm_reg])
            vop("tensor_mul", ev(tmp_ap), od(qr_ap), c3, reads=[qr_reg, cos_reg],
                writes=[tmp_reg])
            vop("tensor_mul", od(tmp_ap), ev(qr_ap), s3, reads=[qr_reg, sin_reg],
                writes=[tmp_reg])
            vop("tensor_add", od(dst_tm_ap), ev(tmp_ap), od(tmp_ap),
                reads=[tmp_reg], writes=[dst_tm_reg])
            vop("tensor_add", od(dst_tm_ap), od(dst_tm_ap), od(qc_ap),
                reads=[dst_tm_reg, qc_reg], writes=[dst_tm_reg])

        def fm_transposes(src_tm_ap, src_reg, dst_buf, tt, S_, banks=(6, 7),
                          gcol=None, bcol=None, gb=None, scale=None):
            for c in range(8):
                bi = banks[c % 2]
                tpose(bi, pB(bi)[:, 0:128], src_tm_ap[:, c * 128:(c + 1) * 128],
                      idb.t[:, :], reads=[src_reg, idb.r])
                dst_ap = dst_buf.t[:, c * S_ + tt * 128: c * S_ + tt * 128 + 128]
                if gb is None and scale is not None:
                    vop("tensor_scalar_mul", dst_ap, pB(bi)[:, 0:128], scale,
                        reads=[pbr[bi]], writes=[dst_buf.r])
                elif gb is None:
                    vop("tensor_copy", dst_ap, pB(bi)[:, 0:128],
                        reads=[pbr[bi]], writes=[dst_buf.r])
                else:
                    g_, b_ = gb
                    vop("tensor_scalar", dst_ap, pB(bi)[:, 0:128],
                        g_.t[:, c:c + 1], b_.t[:, c:c + 1],
                        op0=ALU.mult, op1=ALU.add,
                        reads=[pbr[bi], g_.r, b_.r], writes=[dst_buf.r])

        def tm_transposes(src_buf, tt, S_, dst_ap, dst_reg, banks=(6, 7)):
            for c in range(8):
                bi = banks[c % 2]
                tpose(bi, pB(bi)[:, 0:128],
                      src_buf.t[:, c * S_ + tt * 128: c * S_ + tt * 128 + 128],
                      idb.t[:, :], reads=[src_buf.r, idb.r])
                vop("tensor_copy", dst_ap[:, c * 128:(c + 1) * 128], pB(bi)[:, 0:128],
                    reads=[pbr[bi]], writes=[dst_reg])

        def ctx_finalize(h, sidx, ctx_ap, ctx_reg):
            rh = 64 * (h % 2)
            bi_ctx = 4 + sidx
            den = (sc_b.t[0:1, :], sc_b.r) if sidx == 0 else (sc_b2.t[0:1, :], sc_b2.r)
            cbank = 6 + sidx
            vop("reciprocal", den[0][:, 0:512], pbt[bi_ctx][64:65, 0:512],
                reads=[pbr[bi_ctx]], writes=[den[1]])
            if rh == 0:
                mm(cbank, pbt[cbank][0:64, 0:512], on1f.t[0:1, 0:64],
                   den[0][:, 0:512], True, True, reads=[on1f.r, den[1]])
                vop("tensor_copy", ctmp.t[0:64, 0:512], pbt[bi_ctx][0:64, 0:512],
                    reads=[pbr[bi_ctx]], writes=[ctmp.r])
                vop("tensor_mul", ctx_ap, ctmp.t[0:64, 0:512],
                    pbt[cbank][0:64, 0:512],
                    reads=[ctmp.r, pbr[cbank]], writes=[ctx_reg])
            else:
                vop("tensor_copy", ctbf.t[0:64, 0:512], pbt[bi_ctx][0:64, 0:512],
                    reads=[pbr[bi_ctx]], writes=[ctbf.r])
                mm(cbank, pbt[cbank][64:128, 0:512], idb.t[0:64, 0:64],
                   ctbf.t[0:64, 0:512], True, True, reads=[idb.r, ctbf.r])
                bok = 6 + (1 - sidx)
                mm(bok, pbt[bok][64:128, 0:512], on1f.t[0:1, 0:64],
                   den[0][:, 0:512], True, True, reads=[on1f.r, den[1]])
                vop("tensor_copy", ctmp.t[64:128, 0:512], pbt[cbank][64:128, 0:512],
                    reads=[pbr[cbank]], writes=[ctmp.r])
                vop("tensor_mul", ctx_ap, ctmp.t[64:128, 0:512],
                    pbt[bok][64:128, 0:512],
                    reads=[ctmp.r, pbr[bok]], writes=[ctx_reg])

        def maybe_tap(name, ap, reg):
            if name in tap_d:
                dma("gpsimd", tap_d[name], ap, reads=[reg])

        ctbf = ar.alloc("ctbf", 1024, BF16)

        def ln_alloc():
            return {"zt0": ar.alloc("zt0", 4096, F32),
                    "zt1": ar.alloc("zt1", 4096, F32),
                    "zsq": ar.alloc("zsq", 4096, F32),
                    "nrm0": ar.alloc("nrm0", 2048, BF16),
                    "nrm1": ar.alloc("nrm1", 2048, BF16)}

        for b in range(B):
            xtm = x_in[b].rearrange("(t p) d -> p t d", p=128)
            otm = out_d[b].rearrange("(t p) d -> p t d", p=128)

            for nm_ in ("bu_q", "bq_r", "bu_k", "bk_r", "bu_v", "bo_a"):
                brow[nm_] = ar.alloc("row_" + nm_, 2 * D, BF16, rows=1)
                dma("gpsimd", brow[nm_].t[0:1, :], wd[nm_], writes=[brow[nm_].r])
            xbf = ar.alloc("xbf", 2 * 8 * S, BF16)
            stgA = ar.alloc("stgA", 4 * 4 * D, F32)
            stgB = ar.alloc("stgB", 4 * 4 * D, F32)
            for g4 in range(NTT // 4):
                slot = stgA if g4 % 2 == 0 else stgB
                dma("sync", slot.t[:, :].rearrange("p (t d) -> p t d", t=4),
                    xtm[:, 4 * g4:4 * g4 + 4, :], writes=[slot.r], stream="xr")
                for tl in range(4):
                    tt = 4 * g4 + tl
                    for c in range(8):
                        bi = c % 2
                        tpose(bi, pbt[bi][:, 0:128],
                              slot.t[:, tl * D + c * 128: tl * D + c * 128 + 128],
                              idf.t[:, :], reads=[slot.r, idf.r])
                        vop("tensor_copy",
                            xbf.t[:, c * S + tt * 128: c * S + tt * 128 + 128],
                            pbt[bi][:, 0:128], reads=[pbr[bi]], writes=[xbf.r])
            stgA.free(); stgB.free()

            ctkv = ar.alloc("ctkv", 2 * 4 * S, BF16)
            wsb = ar.alloc("wsb", 2 * 8 * 512, BF16)
            wload(wsb, wd["wd_kv"], D, DKV)

            def ct_proj(dst, wsl, bias):
                for m in range(4):
                    for nb in range(S // 512):
                        bi = (m * (S // 512) + nb) % 4
                        for c in range(8):
                            mm(bi, pbt[bi][:, :],
                               wsl.t[:, c * 512 + m * 128: c * 512 + m * 128 + 128],
                               xbf.t[:, c * S + nb * 512: c * S + nb * 512 + 512],
                               c == 0, c == 7, reads=[wsl.r, xbf.r])
                        act(dst.t[:, m * S + nb * 512: m * S + nb * 512 + 512],
                            pbt[bi][:, :], AF.Identity, bias=bias.t[:, m:m + 1],
                            reads=[pbr[bi], bias.r], writes=[dst.r])

            ct_proj(ctkv, wsb, bdkv)
            wsb.free()

            key = ar.alloc("key", 2 * 8 * S, BF16)
            vtm = ar.alloc("vtm", 2 * NTT * D, BF16)
            wuk = ar.alloc("wuk", 2 * 4 * D, BF16)
            wkr = ar.alloc("wkr", 2 * 8 * D, BF16)
            wload(wuk, wd["wu_k"], DKV, D)
            wload(wkr, wd["wk_r"], D, D)
            def rope_alloc():
                d = {}
                d["mct0"] = ar.alloc("mct0", 2 * 512, BF16)
                d["mct1"] = ar.alloc("mct1", 2 * 512, BF16)
                d["mst0"] = ar.alloc("mst0", 2 * 512, BF16)
                d["mst1"] = ar.alloc("mst1", 2 * 512, BF16)
                d["qcs"] = ar.alloc("qcs", 2 * D, BF16)
                d["qrs"] = ar.alloc("qrs", 2 * D, BF16)
                d["tmps"] = ar.alloc("tmps", 2 * D, BF16)
                d["qts0"] = ar.alloc("qts0", 2 * D, BF16)
                d["qts1"] = ar.alloc("qts1", 2 * D, BF16)
                return d

            rb = rope_alloc()
            mct2, mst2 = [rb["mct0"], rb["mct1"]], [rb["mst0"], rb["mst1"]]
            qcs2 = [rb["qcs"], rb["qcs"]]
            qrs2 = [rb["qrs"], rb["qrs"]]
            qts2 = [rb["qts0"], rb["qts1"]]
            tmps2 = [rb["tmps"], rb["tmps"]]
            costm = td["mla_cos"].rearrange("(t p) f -> p t f", p=128)
            sintm = td["mla_sin"].rearrange("(t p) f -> p t f", p=128)

            def qk_tile(tt, ct_buf, w_c, brow_c, w_r, brow_r, x_lhs, dst_fm):
                sl = tt % 2
                qcs, qrs, qts, tmps = qcs2[sl], qrs2[sl], qts2[sl], tmps2[sl]
                dma("sync", mct2[sl].t[:, :], costm[:, tt, :], writes=[mct2[sl].r], stream="rt%d" % sl)
                dma("sync", mst2[sl].t[:, :], sintm[:, tt, :], writes=[mst2[sl].r], stream="rs%d" % sl)
                for half in range(2):
                    bi = half
                    for kc in range(4):
                        mm(bi, pbt[bi][:, :],
                           ct_buf.t[:, kc * S + tt * 128: kc * S + tt * 128 + 128],
                           w_c.t[:, kc * D + half * 512: kc * D + half * 512 + 512],
                           kc == 0, False, reads=[ct_buf.r, w_c.r])
                    mm(bi, pbt[bi][:, :], on1b.t[0:1, 0:128],
                       brow[brow_c].t[0:1, half * 512:half * 512 + 512],
                       False, True, reads=[on1b.r, brow[brow_c].r])
                    vop("tensor_copy", qcs.t[:, half * 512:half * 512 + 512],
                        pbt[bi][:, :], reads=[pbr[bi]], writes=[qcs.r])
                nkc = 8
                for half in range(2):
                    bi = 2 + half
                    for kc in range(nkc):
                        mm(bi, pbt[bi][:, :],
                           x_lhs.t[:, kc * S + tt * 128: kc * S + tt * 128 + 128],
                           w_r.t[:, kc * D + half * 512: kc * D + half * 512 + 512],
                           kc == 0, False, reads=[x_lhs.r, w_r.r])
                    mm(bi, pbt[bi][:, :], on1b.t[0:1, 0:128],
                       brow[brow_r].t[0:1, half * 512:half * 512 + 512],
                       False, True, reads=[on1b.r, brow[brow_r].r])
                    vop("tensor_copy", qrs.t[:, half * 512:half * 512 + 512],
                        pbt[bi][:, :], reads=[pbr[bi]], writes=[qrs.r])
                rope_combine(tt, qts.t[:, :], qts.r, mct2[sl].t[:, :],
                             mst2[sl].t[:, :],
                             mct2[sl].r, mst2[sl].r, qcs.t[:, :], qcs.r,
                             qrs.t[:, :], qrs.r, tmps.t[:, :], tmps.r)

                def ep_(qts=qts, tt=tt, dst_fm=dst_fm):
                    fm_transposes(qts.t[:, :], qts.r, dst_fm, tt, S)
                return ep_

            pend_t = None
            for tt in range(NTT):
                ep = qk_tile(tt, ctkv, wuk, "bu_k", wkr, "bk_r", xbf, key)
                if pend_t is not None:
                    pend_t()
                pend_t = ep
            pend_t()
            wuk.free(); wkr.free()
            brow["bu_k"].free(); brow["bk_r"].free()
            for v_ in rb.values():
                v_.free()
            wuv = ar.alloc("wuv", 2 * 4 * D, BF16)
            wload(wuv, wd["wu_v"], DKV, D)
            for tt in range(NTT):
                for half in range(2):
                    bi = 4 + half
                    for kc in range(4):
                        mm(bi, pbt[bi][:, :],
                           ctkv.t[:, kc * S + tt * 128: kc * S + tt * 128 + 128],
                           wuv.t[:, kc * D + half * 512: kc * D + half * 512 + 512],
                           kc == 0, False, reads=[ctkv.r, wuv.r])
                    mm(bi, pbt[bi][:, :], on1b.t[0:1, 0:128],
                       brow["bu_v"].t[0:1, half * 512:half * 512 + 512],
                       False, True, reads=[on1b.r, brow["bu_v"].r])
                    vop("tensor_copy",
                        vtm.t[:, tt * D + half * 512: tt * D + half * 512 + 512],
                        pbt[bi][:, :], reads=[pbr[bi]], writes=[vtm.r])
            wuv.free(); brow["bu_v"].free()
            ctkv.free()
            ctq = ar.alloc("ctq", 2 * 4 * S, BF16)
            wsa = ar.alloc("wsa", 2 * 8 * 512, BF16)
            wload(wsa, wd["wd_q"], D, DQ)
            ct_proj(ctq, wsa, bdq)
            wsa.free()
            xbf.free()

            x1h = ar.alloc("x1h", 2 * 8 * S, BF16)
            wuq = ar.alloc("wuq", 2 * 4 * D, BF16)
            wqr = ar.alloc("wqr", 2 * 4 * D, BF16)
            wload(wuq, wd["wu_q"], DQ, D)
            wload(wqr, wd["wq_r"], DQ, D)
            qrot = ar.alloc("qrot", 2 * 8 * 512, BF16)
            ctxq = ar.alloc("ctxq", 2 * 8 * 512, BF16)
            pst["ppc"] = [ar.alloc("ppc%d" % i, 2 * 2 * 512, BF16)
                          for i in range(NPP)]
            pst["rot"] = 0

            for qb in range(NQB):
                rb = rope_alloc()
                mct2, mst2 = [rb["mct0"], rb["mct1"]], [rb["mst0"], rb["mst1"]]
                qcs2 = [rb["qcs"], rb["qcs"]]
                qrs2 = [rb["qrs"], rb["qrs"]]
                qts2 = [rb["qts0"], rb["qts1"]]
                tmps2 = [rb["tmps"], rb["tmps"]]
                pend_t = None
                for tl in range(4):
                    tt = qb * 4 + tl
                    sl = tt % 2
                    qcs, qrs, qts, tmps = qcs2[sl], qrs2[sl], qts2[sl], tmps2[sl]
                    dma("sync", mct2[sl].t[:, :], costm[:, tt, :], writes=[mct2[sl].r], stream="rt%d" % sl)
                    dma("sync", mst2[sl].t[:, :], sintm[:, tt, :], writes=[mst2[sl].r], stream="rs%d" % sl)
                    for w_, brw, dst, bnk in ((wuq, "bu_q", qcs, 0), (wqr, "bq_r", qrs, 2)):
                        for half in range(2):
                            bi = bnk + half
                            for kc in range(4):
                                mm(bi, pbt[bi][:, :],
                                   ctq.t[:, kc * S + tt * 128: kc * S + tt * 128 + 128],
                                   w_.t[:, kc * D + half * 512: kc * D + half * 512 + 512],
                                   kc == 0, False, reads=[ctq.r, w_.r])
                            mm(bi, pbt[bi][:, :], on1b.t[0:1, 0:128],
                               brow[brw].t[0:1, half * 512:half * 512 + 512],
                               False, True, reads=[on1b.r, brow[brw].r])
                            vop("tensor_copy",
                                dst.t[:, half * 512:half * 512 + 512],
                                pbt[bi][:, :], reads=[pbr[bi]], writes=[dst.r])
                    rope_combine(tt, qts.t[:, :], qts.r, mct2[sl].t[:, :],
                                 mst2[sl].t[:, :], mct2[sl].r, mst2[sl].r,
                                 qcs.t[:, :], qcs.r,
                                 qrs.t[:, :], qrs.r, tmps.t[:, :], tmps.r)
                    if pend_t is not None:
                        pend_t()

                    def _epq(qts=qts, tl=tl):
                        fm_transposes(qts.t[:, :], qts.r, qrot, tl, 512,
                                      scale=MLA_SCALE)
                    pend_t = _epq
                pend_t()
                for v_ in rb.values():
                    v_.free()

                def mla_emit(h):
                    rh = 64 * (h % 2)
                    ch = h // 2

                    def emit_mla(kt, bi, _rh=rh, _ch=ch):
                        mm(bi, pbt[bi][:, :],
                           key.t[_rh:_rh + 64, _ch * S + kt * 128: _ch * S + kt * 128 + 128],
                           qrot.t[_rh:_rh + 64, _ch * 512:_ch * 512 + 512],
                           True, True, reads=[key.r, qrot.r])
                    return emit_mla

                for h in range(NH):
                    attn_head(h, NTT, mla_emit(h), vtm.t, vtm.r, h % 2,
                              compute_c=(h == 0))
                    if h > 0:
                        hp = h - 1
                        ctx_finalize(hp, hp % 2,
                                     ctxq.t[64 * (hp % 2):64 * (hp % 2) + 64,
                                            (hp // 2) * 512:(hp // 2) * 512 + 512],
                                     ctxq.r)
                hp = NH - 1
                ctx_finalize(hp, hp % 2,
                             ctxq.t[64 * (hp % 2):64 * (hp % 2) + 64,
                                    (hp // 2) * 512:(hp // 2) * 512 + 512],
                             ctxq.r)
                woa = ar.alloc("woa", 2 * 8 * D, BF16)
                wload(woa, wd["wo_a"], D, D)
                lnb = ln_alloc()
                zt2 = [lnb["zt0"], lnb["zt1"]]
                nrm2 = [lnb["nrm0"], lnb["nrm1"]]
                zsq = lnb["zsq"]
                s12 = [s1v, s1vB]
                vq32 = qrot.t.bitcast(F32)
                xres2 = [Buf("xres0", vq32[:, 0:1024], Reg("xres0"), 0, 0, ar),
                         Buf("xres1", vq32[:, 1024:2048], Reg("xres1"), 0, 0, ar)]
                pend_t = None
                for tl in range(4):
                    tt = qb * 4 + tl
                    sl = tl % 2
                    zt, nrm, xres = zt2[sl], nrm2[sl], xres2[sl]
                    dma("sync", xres.t[:, :], xtm[:, tt, :],
                        writes=[xres.r, qrot.r], stream="xr%d" % sl)
                    for half in range(2):
                        bi = 2 * sl + half
                        for kc in range(8):
                            mm(bi, pbt[bi][:, :],
                               ctxq.t[:, kc * 512 + tl * 128: kc * 512 + tl * 128 + 128],
                               woa.t[:, kc * D + half * 512: kc * D + half * 512 + 512],
                               kc == 0, False, reads=[ctxq.r, woa.r])
                        mm(bi, pbt[bi][:, :], on1b.t[0:1, 0:128],
                           brow["bo_a"].t[0:1, half * 512:half * 512 + 512],
                           False, True, reads=[on1b.r, brow["bo_a"].r])
                        vop("tensor_add", zt.t[:, half * 512:half * 512 + 512],
                            pbt[bi][:, :], xres.t[:, half * 512:half * 512 + 512],
                            reads=[pbr[bi], xres.r], writes=[zt.r])
                    ln_tile(zt.t[:, 0:D], zt.r, D, nrm.t[:, :], nrm.r,
                            zsq, s12[sl])
                    if pend_t is not None:
                        pend_t()

                    def _ep5(nrm=nrm, tt=tt):
                        fm_transposes(nrm.t[:, :], nrm.r, x1h, tt, S)
                    pend_t = _ep5
                pend_t()
                for v_ in lnb.values():
                    v_.free()
                woa.free()
            maybe_tap("x1h", x1h.t[:, :], x1h.r)
            ctq.free(); wuq.free(); wqr.free()
            key.free(); vtm.free()
            ctxq.free()
            for nm_ in ("bu_q", "bq_r", "bo_a"):
                brow[nm_].free()

            tri = ar.alloc("tri", 4096, BF16)
            tload(tri.t[:, :], "tri", tri.r)
            kxl = ar.alloc("kxl", 2 * 8 * S, BF16)
            vxl = ar.alloc("vxl", 2 * NTT * D, BF16)
            wkx = ar.alloc("wkx", 2 * 8 * D, BF16)
            wload(wkx, wd["wqkv"], D, 3 * D, col0=D, colw=D)
            for c in range(8):
                for nb in range(S // 512):
                    bi = (c * (S // 512) + nb) % 4
                    for kc in range(8):
                        mm(bi, pbt[bi][:, :],
                           wkx.t[:, kc * D + c * 128: kc * D + c * 128 + 128],
                           x1h.t[:, kc * S + nb * 512: kc * S + nb * 512 + 512],
                           kc == 0, kc == 7, reads=[wkx.r, x1h.r])
                    act(kxl.t[:, c * S + nb * 512: c * S + nb * 512 + 512],
                        pbt[bi][:, :], AF.Identity, reads=[pbr[bi]], writes=[kxl.r])
            wkx.free()
            wvx = ar.alloc("wvx", 2 * 8 * D, BF16)
            wload(wvx, wd["wqkv"], D, 3 * D, col0=2 * D, colw=D)
            for tt in range(NTT):
                for half in range(2):
                    bi = 2 * (tt % 2) + half
                    for kc in range(8):
                        mm(bi, pbt[bi][:, :],
                           x1h.t[:, kc * S + tt * 128: kc * S + tt * 128 + 128],
                           wvx.t[:, kc * D + half * 512: kc * D + half * 512 + 512],
                           kc == 0, kc == 7, reads=[x1h.r, wvx.r])
                    vop("tensor_copy",
                        vxl.t[:, tt * D + half * 512: tt * D + half * 512 + 512],
                        pbt[bi][:, :], reads=[pbr[bi]], writes=[vxl.r])
            wvx.free()

            ctxf = ar.alloc("ctxf", 2 * 8 * S, BF16)
            snqb = ar.alloc("snqb", 2 * 4 * 512, BF16)
            csqb = ar.alloc("csqb", 2 * 4 * 512, BF16)
            wrsl2 = [ar.alloc("wrs%d" % i, 2 * D, BF16) for i in range(2)]
            csls = [ar.alloc("csl%d" % i, 2 * 8 * 128, BF16) for i in range(2)]
            wtb = ar.alloc("wtb", 2 * 2 * 512, BF16)
            abb = ar.alloc("abb", 2 * 8 * 512, BF16)
            wtbB = ar.alloc("wtbB", 2 * 2 * 512, BF16)
            abbB = ar.alloc("abbB", 2 * 8 * 512, BF16)
            qtilB = Buf("qtilB", qtil.t, Reg("qtilB"), qtil.off, 0, ar)
            vbarB = Buf("vbarB", vbar.t, Reg("vbarB"), vbar.off, 0, ar)
            cft = ar.alloc("cft", 1024, BF16)
            cfg = ar.alloc("cfg", 1024, BF16)
            abg = Buf("abg", abb.t, Reg("abg"), abb.off, 0, ar)
            abgB = Buf("abgB", abbB.t, Reg("abgB"), abbB.off, 0, ar)
            sinq_d = td["xl_sinq"]
            cosq_d = td["xl_cosq"]
            cs_d = td["xl_cs"]
            for qb in range(NQB):
                nkt = 4 * (qb + 1)
                dma("sync", snqb.t[:, :].rearrange("p (c m) -> p c m", c=4),
                    sinq_d[:, qb, :, :], writes=[snqb.r])
                dma("sync", csqb.t[:, :].rearrange("p (c m) -> p c m", c=4),
                    cosq_d[:, qb, :, :], writes=[csqb.r])
                wqsl2 = [ar.alloc("wqs%d" % i, 2 * 8 * 128, BF16) for i in range(2)]
                for c in range(8):
                    bi = c % 2
                    wslot = c % 2
                    dma("gpsimd",
                        wqsl2[wslot].t[:, :].rearrange("p (k m) -> p k m", k=8),
                        wd["wq_t"][:, c, :, :],
                        writes=[wqsl2[wslot].r], stream="wq%d" % wslot)
                    for kc in range(8):
                        mm(bi, pbt[bi][:, :],
                           wqsl2[wslot].t[:, kc * 128: kc * 128 + 128],
                           x1h.t[:, kc * S + qb * 512: kc * S + qb * 512 + 512],
                           kc == 0, kc == 7, reads=[wqsl2[wslot].r, x1h.r])
                    act(qrot.t[:, c * 512:c * 512 + 512], pbt[bi][:, :], AF.Identity,
                        scale=XL_SCALE, reads=[pbr[bi]], writes=[qrot.r])
                wqsl2[0].free(); wqsl2[1].free()

                def xl_prep(h):
                    rh = 64 * (h % 2)
                    ch = h // 2
                    sl = h % 2
                    qt = [qtil, qtilB][sl]
                    vb = [vbar, vbarB][sl]
                    wt_ = [wtb, wtbB][sl]
                    ab_ = [abb, abbB][sl]
                    ag_ = [abg, abgB][sl]
                    cf = cft
                    vop("tensor_scalar_add", qt.t[rh:rh + 64, :],
                        qrot.t[rh:rh + 64, ch * 512:ch * 512 + 512],
                        ubia.t[rh:rh + 64, h:h + 1],
                        reads=[qrot.r, ubia.r], writes=[qt.r])
                    vop("tensor_scalar_add", vb.t[rh:rh + 64, :],
                        qrot.t[rh:rh + 64, ch * 512:ch * 512 + 512],
                        vbia.t[rh:rh + 64, h:h + 1],
                        reads=[qrot.r, vbia.r], writes=[vb.r])
                    dma("gpsimd", wrsl2[sl].t[rh:rh + 64, :],
                        wd["wrT"][64 * h:64 * h + 64, :], writes=[wrsl2[sl].r],
                        stream="wr%d" % sl)
                    for fc in range(4):
                        for j, fcx in enumerate((fc, fc + 4)):
                            bi = 2 * sl + j
                            mm(bi, pbt[bi][:, :],
                               wrsl2[sl].t[rh:rh + 64, fcx * 128:fcx * 128 + 128],
                               vb.t[rh:rh + 64, :], True, True,
                               reads=[wrsl2[sl].r, vb.r])
                        act(wt_.t[:, 0:1024], pp2[sl][:, 0:1024], AF.Identity,
                            reads=[pbr[2 * sl], pbr[2 * sl + 1]], writes=[wt_.r])
                        ws = wt_.t[:, 0:512]
                        wc = wt_.t[:, 512:1024]
                        sq_ = snqb.t[:, fc * 512:fc * 512 + 512]
                        cq_ = csqb.t[:, fc * 512:fc * 512 + 512]
                        Aap = ab_.t[:, fc * 512:fc * 512 + 512]
                        Bap = ab_.t[:, (4 + fc) * 512:(4 + fc) * 512 + 512]
                        vop("tensor_mul", Aap, ws, sq_, reads=[wt_.r, snqb.r],
                            writes=[ab_.r])
                        vop("tensor_mul", cf.t[:, :], wc, cq_,
                            reads=[wt_.r, csqb.r], writes=[cf.r])
                        vop("tensor_add", Aap, Aap, cf.t[:, :],
                            reads=[ab_.r, cf.r], writes=[ab_.r])
                        if fc < 3:
                            gop("tensor_mul", Bap, wc, sq_, reads=[wt_.r, snqb.r],
                                writes=[ag_.r])
                            gop("tensor_mul", cfg.t[:, :], ws, cq_,
                                reads=[wt_.r, csqb.r], writes=[cfg.r])
                            gop("tensor_sub", Bap, Bap, cfg.t[:, :],
                                reads=[ag_.r, cfg.r], writes=[ag_.r])
                        else:
                            vop("tensor_mul", Bap, wc, sq_, reads=[wt_.r, snqb.r],
                                writes=[ag_.r])
                            vop("tensor_mul", cf.t[:, :], ws, cq_,
                                reads=[wt_.r, csqb.r], writes=[cft.r])
                            vop("tensor_sub", Bap, Bap, cf.t[:, :],
                                reads=[ag_.r, cft.r], writes=[ag_.r])

                def xl_emit(h):
                    rh = 64 * (h % 2)
                    ch = h // 2
                    sl = h % 2
                    qt = [qtil, qtilB][sl]
                    ab_ = [abb, abbB][sl]
                    ag_ = [abg, abgB][sl]

                    def emit_xl(kt, bi, _rh=rh, _ch=ch, _qt=qt, _ab=ab_, _ag=ag_):
                        ksl = kt % 2
                        dma("sync", csls[ksl].t[:, :].rearrange("p (c m) -> p c m", c=8),
                            cs_d[:, kt, :, :], writes=[csls[ksl].r],
                            stream="cs%d" % ksl)
                        mm(bi, pbt[bi][:, :],
                           kxl.t[_rh:_rh + 64, _ch * S + kt * 128: _ch * S + kt * 128 + 128],
                           _qt.t[_rh:_rh + 64, :], True, False,
                           reads=[kxl.r, _qt.r])
                        for fc in range(8):
                            mm(bi, pbt[bi][:, :],
                               csls[ksl].t[:, fc * 128:fc * 128 + 128],
                               _ab.t[:, fc * 512:fc * 512 + 512], False, fc == 7,
                               reads=[csls[ksl].r, _ab.r if fc < 4 else _ag.r])
                    return emit_xl

                def xl_fin(hp):
                    ctx_finalize(hp, hp % 2,
                                 ctxf.t[64 * (hp % 2):64 * (hp % 2) + 64,
                                        (hp // 2) * S + qb * 512:
                                        (hp // 2) * S + qb * 512 + 512], ctxf.r)

                xl_prep(0)
                for h in range(NH):
                    if h + 1 < NH:
                        xl_prep(h + 1)
                    attn_head(h, nkt, xl_emit(h), vxl.t, vxl.r, h % 2,
                              diag_from=4 * qb, compute_c=(h == 0))
                    if h > 0:
                        xl_fin(h - 1)
                xl_fin(NH - 1)
            maybe_tap("ctxf", ctxf.t[:, :], ctxf.r)
            snqb.free(); csqb.free(); cfg.free(); cft.free()
            wrsl2[0].free(); wrsl2[1].free()
            for c_ in csls:
                c_.free()
            wtb.free(); abb.free(); wtbB.free(); abbB.free()
            for p_ in pst["ppc"]:
                p_.free()
            pst["ppc"] = None
            kxl.free(); vxl.free()
            qrot.free(); tri.free()

            x1tm2 = [ar.alloc("x1tma", 2048, BF16),
                     ar.alloc("x1tmb", 2048, BF16)]
            lnb = ln_alloc()
            zt2 = [lnb["zt0"], lnb["zt1"]]
            nrm2 = [lnb["nrm0"], lnb["nrm1"]]
            zsq = lnb["zsq"]
            s12 = [s1v, s1vB]
            h1tm4 = ar.alloc("h1tm4", 2 * 4 * D, BF16)
            h1f = ar.alloc("h1f", 2 * 8 * S, BF16)
            wox = ar.alloc("wox", 2 * 8 * D, BF16)
            wload(wox, wd["wo_x"], D, D)
            pend_t = None
            for tt in range(NTT):
                sl = tt % 2
                zt, nrm, x1tm = zt2[sl], nrm2[sl], x1tm2[sl]
                tm_transposes(x1h, tt, S, x1tm.t[:, :], x1tm.r, banks=(4, 5))
                for half in range(2):
                    bi = 2 * sl + half
                    for kc in range(8):
                        mm(bi, pbt[bi][:, :],
                           ctxf.t[:, kc * S + tt * 128: kc * S + tt * 128 + 128],
                           wox.t[:, kc * D + half * 512: kc * D + half * 512 + 512],
                           kc == 0, kc == 7, reads=[ctxf.r, wox.r])
                    vop("tensor_add", zt.t[:, half * 512:half * 512 + 512],
                        pbt[bi][:, :], x1tm.t[:, half * 512:half * 512 + 512],
                        reads=[pbr[bi], x1tm.r], writes=[zt.r])
                ln_tile(zt.t[:, 0:D], zt.r, D, nrm.t[:, :], nrm.r, zsq, s12[sl])
                if pend_t is not None:
                    pend_t()

                def _ep8(nrm=nrm, tt=tt):
                    fm_transposes(nrm.t[:, :], nrm.r, h1f, tt, S,
                                  gb=(lnp["ln1_g"], lnp["ln1_b"]))
                pend_t = _ep8
            pend_t()
            maybe_tap("h1f", h1f.t[:, :], h1f.r)
            ctxf.free(); wox.free()
            x1tm2[0].free(); x1tm2[1].free()

            h2f = ar.alloc("h2f", 2 * 8 * S, BF16)
            brow["bff2"] = ar.alloc("row_bff2", 2 * D, BF16, rows=1)
            dma("gpsimd", brow["bff2"].t[0:1, :], wd["bff2"], writes=[brow["bff2"].r])
            hid = ar.alloc("hid", 2 * 32 * TBW, BF16)
            sls = [ar.alloc("sl_%d" % i, 2 * 8 * 512, BF16) for i in range(3)]
            wff2_d = wd["wff2"].rearrange("(c p) m -> p c m", p=128)

            def ffn_like(w1_ap, w2kc_ap, bias_col0, brow2_key, brow2_off, dst_fn,
                         tb, we_bias_buf):
                pass

            for tb in range(S // TBW):
                for cb in range(8):
                    sslot = cb % 3
                    dma("gpsimd",
                        sls[sslot].t[:, :].rearrange("p (c m) -> p c m", c=8),
                        wd["wff1"].rearrange("(c p) m -> p c m", p=128)[:, :, cb * 512:cb * 512 + 512],
                        writes=[sls[sslot].r], stream="sl%d" % sslot)
                    for fc2 in range(4):
                        fcg = cb * 4 + fc2
                        for nb2 in range(TBW // 512):
                            bi = nb2
                            for kc in range(8):
                                mm(bi, pbt[bi][:, :],
                                   sls[sslot].t[:, kc * 512 + fc2 * 128: kc * 512 + fc2 * 128 + 128],
                                   h1f.t[:, kc * S + tb * TBW + nb2 * 512: kc * S + tb * TBW + nb2 * 512 + 512],
                                   kc == 0, kc == 7, reads=[sls[sslot].r, h1f.r])
                            act(hid.t[:, fcg * TBW + nb2 * 512: fcg * TBW + nb2 * 512 + 512],
                                pbt[bi][:, :], AF.Relu, bias=bff1.t[:, fcg:fcg + 1],
                                reads=[pbr[bi], bff1.r], writes=[hid.r])
                for tl2 in range(TBW // 128):
                    tt = tb * (TBW // 128) + tl2
                    tm_transposes(h1f, tt, S, h1tm4.t[:, tl2 * D:(tl2 + 1) * D],
                                  h1tm4.r, banks=(0, 1))
                for grp in range(8):
                    gslot = grp % 3
                    dma("gpsimd",
                        sls[gslot].t[:, :].rearrange("p (c m) -> p c m", c=4),
                        wff2_d[:, grp * 4:grp * 4 + 4, :], writes=[sls[gslot].r],
                        stream="sl%d" % gslot)
                    for kcl in range(4):
                        kc = grp * 4 + kcl
                        for tl2 in range(TBW // 128):
                            for half in range(2):
                                bi = tl2 * 2 + half
                                mm(bi, pbt[bi][:, :],
                                   hid.t[:, kc * TBW + tl2 * 128: kc * TBW + tl2 * 128 + 128],
                                   sls[gslot].t[:, kcl * D + half * 512: kcl * D + half * 512 + 512],
                                   kc == 0, False, reads=[hid.r, sls[gslot].r])
                pend_t = None
                for tl2 in range(TBW // 128):
                    tt = tb * (TBW // 128) + tl2
                    sl = tl2 % 2
                    zt, nrm = zt2[sl], nrm2[sl]
                    for half in range(2):
                        bi = tl2 * 2 + half
                        mm(bi, pbt[bi][:, :], on1b.t[0:1, 0:128],
                           brow["bff2"].t[0:1, half * 512:half * 512 + 512],
                           False, True, reads=[on1b.r, brow["bff2"].r])
                        vop("tensor_add", zt.t[:, half * 512:half * 512 + 512],
                            pbt[bi][:, :],
                            h1tm4.t[:, tl2 * D + half * 512: tl2 * D + half * 512 + 512],
                            reads=[pbr[bi], h1tm4.r], writes=[zt.r])
                    ln_tile(zt.t[:, 0:D], zt.r, D, nrm.t[:, :], nrm.r,
                            zsq, s12[sl])
                    if pend_t is not None:
                        pend_t()

                    def _ep9(nrm=nrm, tt=tt, tl2=tl2):
                        fm_transposes(nrm.t[:, :], nrm.r, h2f, tt, S,
                                      banks=(tl2 * 2, tl2 * 2 + 1),
                                      gb=(lnp["ln2_g"], lnp["ln2_b"]))
                    pend_t = _ep9
                pend_t()
            maybe_tap("h2f", h2f.t[:, :], h2f.r)
            h1f.free(); brow["bff2"].free(); h1tm4.free()
            for v_ in lnb.values():
                v_.free()

            for tt in range(NTT):
                for kc in range(8):
                    mm(4, pbt[4][0:128, 0:8],
                       h2f.t[:, kc * S + tt * 128: kc * S + tt * 128 + 128],
                       wgb.t[:, kc * 8:kc * 8 + 8], kc == 0, kc == 7,
                       reads=[h2f.r, wgb.r])
                vop("tensor_copy", lgt.t[:, 0:8], pbt[4][0:128, 0:8],
                    reads=[pbr[4]], writes=[lgt.r])
                vop("reduce_max", gsc.t[:, 0:1], lgt.t[:, 0:8], axis=AX.X,
                    negate=True, reads=[lgt.r], writes=[gsc.r])
                act(gsc.t[:, 4:4 + 8], lgt.t[:, 0:8], AF.Exp,
                    bias=gsc.t[:, 0:1], reads=[lgt.r, gsc.r], writes=[gsc.r])
                vop("reduce_sum", gsc.t[:, 1:2], gsc.t[:, 4:12], axis=AX.X,
                    reads=[gsc.r], writes=[gsc.r])
                vop("reciprocal", gsc.t[:, 2:3], gsc.t[:, 1:2],
                    reads=[gsc.r], writes=[gsc.r])
                vop("tensor_scalar_mul", gsc.t[:, 3:4], gsc.t[:, 0:1], -1.0,
                    reads=[gsc.r], writes=[gsc.r])
                vop("tensor_scalar", mgt.t[:, tt * 8:tt * 8 + 8], lgt.t[:, 0:8],
                    gsc.t[:, 3:4], gsc.t[:, 2:3], op0=ALU.is_equal, op1=ALU.mult,
                    reads=[lgt.r, gsc.r], writes=[mgt.r])
            maybe_tap("mgt", mgt.t[:, :], mgt.r)

            TB2 = 1024 if S >= 1024 else S
            hid.free()
            moe = ar.alloc("moe", 2 * NTT * D, BF16)
            vop("memset", moe.t[:, :], 0.0, writes=[moe.r])
            hid2p = [ar.alloc("hid2_%d" % i, 2 * 4 * TB2, BF16) for i in range(8)]

            def hid2_ap(col0, width):
                buf = hid2p[col0 // (4 * TB2)]
                off = col0 % (4 * TB2)
                return buf.t[:, off:off + width], buf
            brow["be2"] = ar.alloc("row_be2", 2 * D, BF16, rows=1)
            for e in range(NE):
                dma("gpsimd", brow["be2"].t[0:1, :],
                    wd["be2"][:, e * D:(e + 1) * D], writes=[brow["be2"].r],
                    stream="be2")
                we1_d = wd["we1"][e]
                we2_d = wd["we2"][e].rearrange("(c p) m -> p c m", p=128)
                for tb in range(S // TB2):
                    for cb in range(8):
                        sslot = cb % 3
                        dma("gpsimd",
                            sls[sslot].t[:, :].rearrange("p (c m) -> p c m", c=8),
                            we1_d.rearrange("(c p) m -> p c m", p=128)[:, :, cb * 512:cb * 512 + 512],
                            writes=[sls[sslot].r], stream="sl%d" % sslot)
                        for fc2 in range(4):
                            fcg = cb * 4 + fc2
                            for nb2 in range(TB2 // 512):
                                bi = nb2 % 2
                                for kc in range(8):
                                    mm(bi, pbt[bi][:, :],
                                       sls[sslot].t[:, kc * 512 + fc2 * 128: kc * 512 + fc2 * 128 + 128],
                                       h2f.t[:, kc * S + tb * TB2 + nb2 * 512: kc * S + tb * TB2 + nb2 * 512 + 512],
                                       kc == 0, kc == 7, reads=[sls[sslot].r, h2f.r])
                                hap, hbuf = hid2_ap(fcg * TB2 + nb2 * 512, 512)
                                act(hap, pbt[bi][:, :], AF.Relu,
                                    bias=be1.t[:, e * 32 + fcg:e * 32 + fcg + 1],
                                    reads=[pbr[bi], be1.r], writes=[hbuf.r])
                    for tg in range(TB2 // 512):
                        for grp in range(8):
                            gslot = grp % 3
                            dma("gpsimd",
                                sls[gslot].t[:, :].rearrange("p (c m) -> p c m", c=4),
                                we2_d[:, grp * 4:grp * 4 + 4, :], writes=[sls[gslot].r],
                                stream="sl%d" % gslot)
                            for kcl in range(4):
                                kc = grp * 4 + kcl
                                for tl in range(4):
                                    for half in range(2):
                                        bi = tl * 2 + half
                                        tloc = tg * 4 + tl
                                        hap, hbuf = hid2_ap(kc * TB2 + tloc * 128, 128)
                                        mm(bi, pbt[bi][:, :], hap,
                                           sls[gslot].t[:, kcl * D + half * 512: kcl * D + half * 512 + 512],
                                           kc == 0, False, reads=[hbuf.r, sls[gslot].r])
                        for tl in range(4):
                            tt = tb * (TB2 // 128) + tg * 4 + tl
                            for half in range(2):
                                bi = tl * 2 + half
                                mm(bi, pbt[bi][:, :], on1b.t[0:1, 0:128],
                                   brow["be2"].t[0:1, half * 512: half * 512 + 512],
                                   False, True, reads=[on1b.r, brow["be2"].r])
                                vop("tensor_scalar_mul", ctbf.t[:, :], pbt[bi][:, :],
                                    mgt.t[:, tt * 8 + e:tt * 8 + e + 1],
                                    reads=[pbr[bi], mgt.r], writes=[ctbf.r])
                                vop("tensor_add",
                                    moe.t[:, tt * D + half * 512: tt * D + half * 512 + 512],
                                    moe.t[:, tt * D + half * 512: tt * D + half * 512 + 512],
                                    ctbf.t[:, :], reads=[moe.r, ctbf.r], writes=[moe.r])
            for h_ in hid2p:
                h_.free()
            brow["be2"].free()
            for s_ in sls:
                s_.free()
            maybe_tap("moe", moe.t[:, :], moe.r)

            x1tm2 = [ar.alloc("x1tma", 2048, BF16),
                     ar.alloc("x1tmb", 2048, BF16)]
            h1tm2 = [ar.alloc("h1tma", 2048, BF16),
                     ar.alloc("h1tmb", 2048, BF16)]
            lnb = ln_alloc()
            zt2 = [lnb["zt0"], lnb["zt1"]]
            nrm2 = [lnb["nrm0"], lnb["nrm1"]]
            zsq = lnb["zsq"]
            s12 = [s1v, s1vB]
            outf2 = [ar.alloc("outf0", 4096, F32),
                     ar.alloc("outf1", 4096, F32)]
            wot = ar.alloc("wot", 2 * 8 * D, BF16)
            wload(wot, wd["wout"], D, D)
            h3fs2 = [ar.alloc("h3fs0", 2 * 8 * 128, BF16),
                     ar.alloc("h3fs1", 2 * 8 * 128, BF16)]
            pend_b = None
            for tt in range(NTT):
                sl = tt % 2
                zt, nrm, h1tm = zt2[sl], nrm2[sl], h1tm2[sl]
                tm_transposes(h2f, tt, S, h1tm.t[:, :], h1tm.r, banks=(0, 1))
                vop("tensor_add", zt.t[:, 0:D], h1tm.t[:, :],
                    moe.t[:, tt * D:tt * D + D], reads=[h1tm.r, moe.r],
                    writes=[zt.r])
                ln_tile(zt.t[:, 0:D], zt.r, D, nrm.t[:, :], nrm.r, zsq, s12[sl])
                if pend_b is not None:
                    pend_b()

                def _epb(tt=tt, sl=sl):
                    nrm, zt = nrm2[sl], zt2[sl]
                    h3fs, x1tm, outf = h3fs2[sl], x1tm2[sl], outf2[sl]
                    for c in range(8):
                        bi = (2, 3)[c % 2]
                        tpose(bi, pB(bi)[:, 0:128], nrm.t[:, c * 128:(c + 1) * 128],
                              idb.t[:, :], reads=[nrm.r, idb.r])
                        vop("tensor_scalar", h3fs.t[:, c * 128:(c + 1) * 128],
                            pB(bi)[:, 0:128], lnp["ln3_g"].t[:, c:c + 1],
                            lnp["ln3_b"].t[:, c:c + 1], op0=ALU.mult, op1=ALU.add,
                            reads=[pbr[bi], lnp["ln3_g"].r, lnp["ln3_b"].r],
                            writes=[h3fs.r])
                    tm_transposes(x1h, tt, S, x1tm.t[:, :], x1tm.r, banks=(6, 7))
                    for half in range(2):
                        bi = 4 + half
                        for kc in range(8):
                            mm(bi, pbt[bi][:, :],
                               h3fs.t[:, kc * 128:kc * 128 + 128],
                               wot.t[:, kc * D + half * 512: kc * D + half * 512 + 512],
                               kc == 0, kc == 7, reads=[h3fs.r, wot.r])
                        vop("tensor_add", zt.t[:, half * 512:half * 512 + 512],
                            pbt[bi][:, :], x1tm.t[:, half * 512:half * 512 + 512],
                            reads=[pbr[bi], x1tm.r], writes=[zt.r])
                    ln_tile(zt.t[:, 0:D], zt.r, D, outf.t[:, :], outf.r,
                            zsq, s12[sl])
                    dma("sync", otm[:, tt, :], outf.t[:, :], reads=[outf.r],
                        stream="out%d" % sl)
                pend_b = _epb
            pend_b()
            wot.free(); moe.free()
            for b_ in h3fs2 + x1tm2 + h1tm2 + outf2:
                b_.free()
            for v_ in lnb.values():
                v_.free()
            x1h.free(); h2f.free()

        kb.final_waits()
        with nc.Block() as blk:
            kb.run(blk)
    return nc


_PROG_CACHE = {}


def kernel(**inputs):
    x = np.asarray(inputs["L0_attn_x"], np.float32)
    BTOT, S, _ = x.shape
    BPC = BTOT // NCORES
    key = (BPC, S)
    if key not in _PROG_CACHE:
        _PROG_CACHE[key] = build_program(BPC, S)
    nc = _PROG_CACHE[key]
    w = prep_weights(inputs)
    t = host_tables(S)
    base = {}
    for k, v in w.items():
        base["w_" + k] = v
    for k, v in t.items():
        base["t_" + k] = v
    in_maps = []
    for c in range(NCORES):
        m = dict(base)
        m["x"] = np.ascontiguousarray(x[c * BPC:(c + 1) * BPC])
        in_maps.append(m)
    res = run_bass_kernel_spmd(nc, in_maps, core_ids=list(range(NCORES)))
    return np.concatenate([np.asarray(r["out"], np.float32) for r in res.results],
                          axis=0)
